# version 06
# baseline (speedup 1.0000x reference)
import numpy as np
import ml_dtypes
import contextlib
import concourse.bass as bass
import concourse.mybir as mybir

F32 = mybir.dt.float32
BF16 = mybir.dt.bfloat16
I32 = mybir.dt.int32
AF = mybir.ActivationFunctionType
ALU = mybir.AluOpType
AX = mybir.AxisListType.X

D = 4096
T = 2048
NOWN = 1024
DC = 32
NH = 32
KVH = 8
HD = 128
NI = 11008
IC = 86
QSZ = [22, 22, 21, 21]
QBASE = [0, 22, 44, 65]
SCALE = float(1.0 / np.sqrt(HD))
EPS = 1e-5
NPAIRS = [4, 8]


class Sem:
    def __init__(self, h):
        self.h = h
        self.v = 0


def build_nc(debug=False):
    nc = bass.Bass("TRN2", target_bir_lowering=False)

    x_seq = nc.dram_tensor("x_seq", [T, D], F32, kind="ExternalInput").ap()
    x_ownT = nc.dram_tensor("x_ownT", [D, NOWN], F32, kind="ExternalInput").ap()
    wq = nc.dram_tensor("wq", [D, D], F32, kind="ExternalInput").ap()
    wk = nc.dram_tensor("wk", [D, KVH * HD], F32, kind="ExternalInput").ap()
    wv = nc.dram_tensor("wv", [D, KVH * HD], F32, kind="ExternalInput").ap()
    wo = nc.dram_tensor("wo", [D, D], F32, kind="ExternalInput").ap()
    wgT = nc.dram_tensor("wgT", [D, NI], I32, kind="ExternalInput").ap()
    wuT = nc.dram_tensor("wuT", [D, NI], I32, kind="ExternalInput").ap()
    wdT = nc.dram_tensor("wdT", [NI, D], I32, kind="ExternalInput").ap()
    sgT = nc.dram_tensor("sgT", [128, IC], F32, kind="ExternalInput").ap()
    suT = nc.dram_tensor("suT", [128, IC], F32, kind="ExternalInput").ap()
    sdT = nc.dram_tensor("sdT", [128, DC], F32, kind="ExternalInput").ap()
    cosq = nc.dram_tensor("cosq", [64, NOWN], BF16, kind="ExternalInput").ap()
    sinq = nc.dram_tensor("sinq", [64, NOWN], BF16, kind="ExternalInput").ap()
    cosk = nc.dram_tensor("cosk", [64, T], BF16, kind="ExternalInput").ap()
    sink = nc.dram_tensor("sink", [64, T], BF16, kind="ExternalInput").ap()
    masks = nc.dram_tensor("masks", [128, 256], BF16, kind="ExternalInput").ap()
    iden_in = nc.dram_tensor("iden", [128, 128], BF16, kind="ExternalInput").ap()
    ones_in = nc.dram_tensor("ones2", [128, 128], BF16, kind="ExternalInput").ap()

    out_T = nc.dram_tensor("out_T", [D, NOWN], F32, kind="ExternalOutput").ap()

    dbg = "ExternalOutput" if debug else "Internal"
    kT_hbm = nc.dram_tensor("kT_hbm", [KVH, 128, T], BF16, kind=dbg).ap()
    v_hbm = nc.dram_tensor("v_hbm", [KVH, 16, 128, 128], BF16, kind=dbg).ap()
    sT_hbm = nc.dram_tensor("sT_hbm", [D, NOWN], BF16, kind=dbg).ap()
    sp_hbm = nc.dram_tensor("sp_hbm", [2, D, NOWN], BF16, kind=dbg).ap()
    if debug:
        xn_dbg = nc.dram_tensor("xn_dbg", [128, DC * NOWN], BF16, kind="ExternalOutput").ap()
        ctx_dbg = nc.dram_tensor("ctx_dbg", [128, NH * NOWN], BF16, kind="ExternalOutput").ap()
        sffn_dbg = nc.dram_tensor("sffn_dbg", [128, DC * NOWN], BF16, kind="ExternalOutput").ap()

    with contextlib.ExitStack() as _stack:
        ec = _stack.enter_context
        R1 = ec(nc.sbuf_tensor("R1", [128, 32 * 1024], BF16))
        R2 = ec(nc.sbuf_tensor("R2", [128, 32 * 1024], BF16))
        R3 = ec(nc.sbuf_tensor("R3", [128, 22 * 1024], BF16))
        WB = ec(nc.sbuf_tensor("WB", [128, 16 * 1024], BF16))
        mask_t = ec(nc.sbuf_tensor("mask_t", [128, 256], BF16))
        iden_t = ec(nc.sbuf_tensor("iden_t", [128, 128], BF16))
        ones_t = ec(nc.sbuf_tensor("ones_t", [128, 128], BF16))
        sgT_t = ec(nc.sbuf_tensor("sgT_t", [128, IC], F32))
        suT_t = ec(nc.sbuf_tensor("suT_t", [128, IC], F32))
        sdT_t = ec(nc.sbuf_tensor("sdT_t", [128, DC], F32))
        stats = ec(nc.sbuf_tensor("stats", [128, 12], F32))
        PS = ec(nc.psum_tensor("PS", [128, 4096], F32))
        sem_names = ["s_pe", "s_ac", "s_dv", "s_cst", "s_x0", "s_x1",
                     "s_kv0", "s_kv1", "s_kst", "s_vst", "s_wq0", "s_wq1",
                     "s_kk0", "s_kk1", "s_kvv", "s_wd0", "s_wd1", "s_spv",
                     "s_sp0", "s_sp1", "s_ss0", "s_ss1", "s_f0", "s_f1",
                     "s_f2", "s_f3", "s_out", "s_ou1", "s_ou2", "s_ou3",
                     "s_wd2", "s_sp2", "s_sp3", "s_ck", "s_dbg"]
        SM = {n: Sem(ec(nc.semaphore(n))) for n in sem_names}
        block = ec(nc.Block())

        s_pe, s_ac, s_dv = SM["s_pe"], SM["s_ac"], SM["s_dv"]
        s_cst = SM["s_cst"]
        s_x = [SM["s_x0"], SM["s_x1"]]
        s_kv = [SM["s_kv0"], SM["s_kv1"]]
        s_kst, s_vst = SM["s_kst"], SM["s_vst"]
        s_wq = [SM["s_wq0"], SM["s_wq1"]]
        s_kk = [SM["s_kk0"], SM["s_kk1"]]
        s_kvv = SM["s_kvv"]
        s_wd = [SM["s_wd0"], SM["s_wd1"], SM["s_wd2"]]
        s_spv = SM["s_spv"]
        s_sp = [SM["s_sp0"], SM["s_sp1"], SM["s_sp2"], SM["s_sp3"]]
        s_ss = [SM["s_ss0"], SM["s_ss1"]]
        s_f = [SM["s_f0"], SM["s_f1"], SM["s_f2"], SM["s_f3"]]
        s_o = [SM["s_out"], SM["s_ou1"], SM["s_ou2"], SM["s_ou3"]]
        s_ck, s_dbg = SM["s_ck"], SM["s_dbg"]

        prog = {"sync": [], "gpsimd": [], "tensor": [], "vector": [], "scalar": []}
        qlast = {}

        def emit(eng, fn, tag=("o",)):
            prog[eng].append(fn)
            qlast[eng] = tag

        def dma(eng_name, out, in_, sem):
            if qlast.get(eng_name) != ("d", id(sem)) and sem.v > 0:
                emit(eng_name, lambda eng, val=sem.v: eng.wait_ge(sem.h, val))
            sem.v += 16
            v = sem.v
            emit(eng_name, lambda eng, out=out, in_=in_: eng.dma_start(out, in_).then_inc(sem.h, 16),
                 tag=("d", id(sem)))
            return v

        def wait(eng_name, sem, val):
            if val > 0:
                emit(eng_name, lambda eng, val=val: eng.wait_ge(sem.h, val))

        def op(eng_name, fn, sem=None):
            if sem is None:
                emit(eng_name, lambda eng, fn=fn: fn(eng))
                return None
            sem.v += 1
            v = sem.v
            emit(eng_name, lambda eng, fn=fn: fn(eng).then_inc(sem.h, 1))
            return v

        def chain(eng_name, fns, sem):
            v = None
            for fn in fns:
                wait(eng_name, sem, sem.v)
                v = op(eng_name, fn, sem)
            return v

        def group(eng_name, fns, sem):
            sem.v += 1
            v = sem.v
            for i, fn in enumerate(fns):
                if i == len(fns) - 1:
                    emit(eng_name, lambda eng, fn=fn: fn(eng).then_inc(sem.h, 1))
                else:
                    emit(eng_name, lambda eng, fn=fn: fn(eng))
            return v

        xn_ownT = R1
        s_ffnT = R1
        xn_qT = [R2[:, 0:DC * 512], R2[:, DC * 512:DC * 1024]]
        ctxT = R2
        hT = R2
        R3f = R3.bitcast(F32)
        xhalf = [[R3[:, 0:2048], R3[:, 2048:4096]],
                 [R3[:, 4096:6144], R3[:, 6144:8192]]]
        xntile = R3[:, 8192:12288]
        sqout = R3[:, 8192:10240]
        sqout2 = R3[:, 10240:12288]
        evk = R3[:, 12288:12800]
        evv = R3[:, 12800:13312]
        rk1 = R3[0:64, 13312:13824]
        rk2 = R3[0:64, 13824:14336]
        ck_s = R3[0:64, 14336:14848]
        sk_s = R3[0:64, 14848:15360]
        PT = R3[:, 0:8192]
        PTr = PT.rearrange("p (c q) -> p c q", q=512)
        kv_k = [R3[:, 8192:10240], R3[:, 10240:12288]]
        kv_v = R3[:, 12288:14336]
        qbf = R3[:, 14336:16384]
        cq_t = R3[0:64, 16384:17408]
        sq_t = R3[0:64, 17408:18432]
        rq1 = R3[0:64, 18432:19456]
        rq2 = R3[0:64, 19456:20480]
        rbc = R3f[:, 10240:10752]
        stile = R3[:, 19456:21504]
        wdb = [R3[:, 0:5632], R3[:, 5632:11264], R2[:, 22528:28160]]
        spv = R3[:, 11264:15360]
        gact = R3[:, 15360:17408]
        us_b = R3[:, 17408:19456]
        R2f = R2.bitcast(F32)
        fxb2 = [R2f[:, 14080 + i * 1024:14080 + (i + 1) * 1024] for i in range(2)]
        fsb2 = [R3[:, 19456 + i * 1024:19456 + (i + 1) * 1024] for i in range(2)]
        fgb2 = [R3f[:, 7680 + i * 1024:7680 + (i + 1) * 1024] for i in range(2)]

        def bank(i):
            return PS[:, i * 512:(i + 1) * 512]
        PSb = PS.bitcast(BF16)

        wk_r = wk.rearrange("(c p) h -> p c h", p=128)
        wv_r = wv.rearrange("(c p) h -> p c h", p=128)
        wq_r = wq.rearrange("(c p) h -> p c h", p=128)
        wo_r = wo.rearrange("(c p) h -> p c h", p=128)
        wgT_r = wgT.rearrange("(c p) i -> p c i", p=128)
        wuT_r = wuT.rearrange("(c p) i -> p c i", p=128)
        wdT_r = wdT.rearrange("(c p) h -> p c h", p=128)
        x_blocks = x_seq.rearrange("(b p) d -> b p d", p=128)

        wkb = [WB[:, 0:4096], WB[:, 4096:8192]]
        wvb = [WB[:, 8192:12288], WB[:, 12288:16384]]
        wqb = [WB[:, 0:4096], WB[:, 4096:8192]]
        wob = wqb
        wgb = [WB[:, 0:4096], WB[:, 4096:8192]]
        wub = [WB[:, 8192:12288], WB[:, 12288:16384]]

        for dst, src in ((mask_t[:], masks), (iden_t[:], iden_in), (ones_t[:], ones_in),
                         (sgT_t[:], sgT), (suT_t[:], suT), (sdT_t[:], sdT)):
            dma("sync", dst, src, s_cst)
        v_const = s_cst.v
        wait("tensor", s_cst, v_const)
        wait("scalar", s_cst, v_const)
        wait("vector", s_cst, v_const)

        a_norm_by = {}
        v_tr_by = {}
        v_kmm_by = {}
        v_vmm_by = {}
        v_krope_by = {}
        a_vraw_by = {}
        evac_v_bank = {}
        A_state = {"kv_iter": 0, "v_kdma": None, "v_vt": None, "v_vev": None,
                   "v_vdma": None, "v_cksk": 0, "evac_v": None, "xn_ready": {}}

        def emit_tile(gb):
            qt = gb // 4
            tb = gb % 4
            xb = xhalf[gb % 2]
            sx = s_x[gb % 2]
            sb6 = (gb % 2) * 6
            st1 = stats[:, sb6 + 0:sb6 + 1]
            st2 = stats[:, sb6 + 1:sb6 + 2]
            st3 = stats[:, sb6 + 2:sb6 + 3]
            st4 = stats[:, sb6 + 3:sb6 + 4]
            mean_t = stats[:, sb6 + 4:sb6 + 5]
            rstd_t = stats[:, sb6 + 5:sb6 + 6]
            if gb >= 2:
                wait("gpsimd", s_dv, a_norm_by[gb - 2])
            dma("gpsimd", xb[0], x_blocks[gb, :, 0:2048], sx)
            vh1 = dma("gpsimd", xb[1], x_blocks[gb, :, 2048:4096], sx)

            wait("vector", sx, vh1)
            chain("vector", [
                lambda eng, xb=xb, st1=st1: eng.reduce_sum(st1, xb[0], axis=AX),
                lambda eng, xb=xb, st2=st2: eng.reduce_sum(st2, xb[1], axis=AX)], s_dv)
            wait("scalar", sx, vh1)
            if gb >= 1:
                wait("scalar", s_pe, v_tr_by[gb - 1])
            a_sq = chain("scalar", [
                lambda eng, xb=xb, st3=st3: eng.activation(sqout[:], xb[0], AF.Square, accum_out=st3),
                lambda eng, xb=xb, st4=st4: eng.activation(sqout2[:], xb[1], AF.Square, accum_out=st4)], s_ac)
            wait("vector", s_ac, a_sq)
            v_var = chain("vector", [
                lambda eng, st1=st1, st2=st2: eng.tensor_add(st1, st1, st2),
                lambda eng, st1=st1, mean_t=mean_t: eng.tensor_scalar(mean_t, st1, 1.0 / D, None, ALU.mult),
                lambda eng, st3=st3, st4=st4: eng.tensor_add(st3, st3, st4),
                lambda eng, st3=st3: eng.tensor_scalar(st3, st3, 1.0 / D, None, ALU.mult),
                lambda eng, st2=st2, mean_t=mean_t: eng.tensor_mul(st2, mean_t, mean_t),
                lambda eng, st3=st3, st2=st2: eng.tensor_scalar(st3, st3, st2, float(EPS), ALU.subtract, ALU.add)], s_dv)
            wait("scalar", s_dv, v_var)
            a_rstd = chain("scalar", [
                lambda eng, st3=st3, st4=st4: eng.activation(st4, st3, AF.Ln),
                lambda eng, st4=st4, rstd_t=rstd_t: eng.activation(rstd_t, st4, AF.Exp, scale=-0.5)], s_ac)
            wait("vector", s_ac, a_rstd)
            if gb >= 1:
                wait("vector", s_pe, v_tr_by[gb - 1])
            a_norm = chain("vector", [
                lambda eng, xb=xb, mean_t=mean_t, rstd_t=rstd_t: eng.tensor_scalar(xntile[:, 0:2048], xb[0], mean_t, rstd_t, ALU.subtract, ALU.mult),
                lambda eng, xb=xb, mean_t=mean_t, rstd_t=rstd_t: eng.tensor_scalar(xntile[:, 2048:4096], xb[1], mean_t, rstd_t, ALU.subtract, ALU.mult)], s_dv)
            a_norm_by[gb] = a_norm

            wait("tensor", s_dv, a_norm)
            v_tr = None
            for cg in range(4):
                bk = cg % 2
                if bk in evac_v_bank:
                    wait("tensor", s_dv, evac_v_bank[bk])
                fns = []
                for cc in range(8):
                    c = cg * 8 + cc
                    fns.append(lambda eng, bk=bk, cc=cc, c=c: eng.transpose(
                        PSb[:, bk * 1024 + cc * 128: bk * 1024 + (cc + 1) * 128],
                        xntile[:, c * 128:(c + 1) * 128], iden_t[:]))
                v_tr = group("tensor", fns, s_pe)
                wait("vector", s_pe, v_tr)
                dst = xn_qT[qt % 2].rearrange("p (c t) -> p c t", t=512)[:, cg * 8:cg * 8 + 8, tb * 128:(tb + 1) * 128]
                src = PSb[:, bk * 1024: bk * 1024 + 1024].rearrange("p (c t) -> p c t", t=128)
                if gb % 2 == 1:
                    op("vector", lambda eng, dst=dst, src=src: eng.tensor_copy(dst, src))
                    dsto = xn_ownT.rearrange("p (c t) -> p c t", t=1024)[:, cg * 8:cg * 8 + 8, (gb // 2) * 128:(gb // 2) * 128 + 128]
                    evac_v = op("vector", lambda eng, dsto=dsto, src=src: eng.tensor_copy(dsto, src), s_dv)
                else:
                    evac_v = op("vector", lambda eng, dst=dst, src=src: eng.tensor_copy(dst, src), s_dv)
                evac_v_bank[bk] = evac_v
            v_tr_by[gb] = v_tr
            if tb == 3:
                A_state["xn_ready"][qt] = evac_v

        def emit_kv(qt, h):
            kv_iter = A_state["kv_iter"]
            bsel = kv_iter % 2
            skv = s_kv[bsel]
            if kv_iter >= 2:
                wait("gpsimd", s_pe, v_vmm_by[kv_iter - 2])
            dma("gpsimd", wkb[bsel].rearrange("p (c j) -> p c j", j=128),
                wk_r[:, :, h * 128:(h + 1) * 128], skv)
            vv_ = dma("gpsimd", wvb[bsel].rearrange("p (c j) -> p c j", j=128),
                      wv_r[:, :, h * 128:(h + 1) * 128], skv)
            kb = 2 + kv_iter % 2
            vb = 4 + kv_iter % 2
            wait("tensor", skv, vv_)
            wait("tensor", s_dv, A_state["xn_ready"][qt])
            if kv_iter >= 2:
                wait("tensor", s_dv, v_krope_by[kv_iter - 2])
            fns = [lambda eng, d=d, kb=kb, bsel=bsel, qt=qt: eng.matmul(
                bank(kb), wkb[bsel][:, d * 128:(d + 1) * 128],
                xn_qT[qt % 2][:, d * 512:(d + 1) * 512],
                start=(d == 0), stop=(d == DC - 1)) for d in range(DC)]
            v_kmm = group("tensor", fns, s_pe)
            v_kmm_by[kv_iter] = v_kmm
            wait("vector", s_pe, v_kmm)
            wait("vector", s_ck, A_state["v_cksk"])
            if A_state["v_kdma"] is not None:
                wait("vector", s_kst, A_state["v_kdma"])
            kq = bank(kb)
            v_krope = chain("vector", [
                lambda eng, kq=kq: eng.tensor_mul(rk1, kq[0:64, :], ck_s),
                lambda eng, kq=kq: eng.tensor_mul(rk2, kq[64:128, :], sk_s),
                lambda eng: eng.tensor_sub(evk[0:64, :], rk1, rk2),
                lambda eng, kq=kq: eng.tensor_mul(rk1, kq[0:64, :], sk_s),
                lambda eng, kq=kq: eng.tensor_mul(rk2, kq[64:128, :], ck_s),
                lambda eng: eng.tensor_add(evk[64:128, :], rk1, rk2)], s_dv)
            v_krope_by[kv_iter] = v_krope
            wait("sync", s_dv, v_krope)
            A_state["v_kdma"] = dma("sync", kT_hbm[h, :, qt * 512:(qt + 1) * 512], evk[:], s_kst)

            if kv_iter >= 2:
                wait("tensor", s_ac, a_vraw_by[kv_iter - 2])
            fns = [lambda eng, d=d, vb=vb, bsel=bsel, qt=qt: eng.matmul(
                bank(vb), wvb[bsel][:, d * 128:(d + 1) * 128],
                xn_qT[qt % 2][:, d * 512:(d + 1) * 512],
                start=(d == 0), stop=(d == DC - 1)) for d in range(DC)]
            v_vmm = group("tensor", fns, s_pe)
            v_vmm_by[kv_iter] = v_vmm
            wait("scalar", s_pe, v_vmm)
            if A_state["v_vt"] is not None:
                wait("scalar", s_pe, A_state["v_vt"])
            if A_state["v_vdma"] is not None:
                wait("scalar", s_vst, A_state["v_vdma"])
            a_vraw = op("scalar", lambda eng, vb=vb: eng.activation(evv[:], bank(vb), AF.Copy), s_ac)
            a_vraw_by[kv_iter] = a_vraw
            wait("tensor", s_ac, a_vraw)
            if A_state["v_vev"] is not None:
                wait("tensor", s_dv, A_state["v_vev"])
            fns = [lambda eng, j=j: eng.transpose(
                PSb[:, 6 * 1024 + j * 128:6 * 1024 + (j + 1) * 128],
                evv[:, j * 128:(j + 1) * 128], iden_t[:]) for j in range(4)]
            v_vt = group("tensor", fns, s_pe)
            A_state["v_vt"] = v_vt
            wait("vector", s_pe, v_vt)
            if A_state["v_vdma"] is not None:
                wait("vector", s_vst, A_state["v_vdma"])
            v_vev = op("vector", lambda eng: eng.tensor_copy(evv[:], PSb[:, 6 * 1024:6 * 1024 + 512]), s_dv)
            A_state["v_vev"] = v_vev
            wait("sync", s_dv, v_vev)
            A_state["v_vdma"] = dma("sync", v_hbm[h, qt * 4:(qt + 1) * 4, :, :].rearrange("j p t -> p j t"),
                                    evv[:].rearrange("p (j t) -> p j t", t=128), s_vst)
            A_state["kv_iter"] += 1

        dma("sync", ck_s, cosk[:, 0:512], s_ck)
        A_state["v_cksk"] = dma("sync", sk_s, sink[:, 0:512], s_ck)
        for tb in range(4):
            emit_tile(tb)
        for qt in range(4):
            if qt >= 1:
                wait("sync", s_dv, v_krope_by[A_state["kv_iter"] - 1])
                dma("sync", ck_s, cosk[:, qt * 512:(qt + 1) * 512], s_ck)
                A_state["v_cksk"] = dma("sync", sk_s, sink[:, qt * 512:(qt + 1) * 512], s_ck)
            for h in range(KVH):
                emit_kv(qt, h)
                if qt < 3 and h % 2 == 1:
                    emit_tile(4 * (qt + 1) + (h - 1) // 2)
        kv_iter = A_state["kv_iter"]
        wdb0_q = R3[:, 0:4096]
        wait("gpsimd", s_dv, a_norm_by[14])
        v_wq0pre = dma("gpsimd", wdb0_q.rearrange("p (c j) -> p c j", j=128),
                       wq_r[:, :, 0:128], s_wd[0])
        v_xndbg = 0
        if debug:
            wait("sync", s_dv, s_dv.v)
            v_xndbg = dma("sync", xn_dbg, xn_ownT[:], s_dbg)

        wait("tensor", s_dv, s_dv.v)
        wait("tensor", s_ac, s_ac.v)
        v_wq = [0, 0]
        v_wo = [0, 0]
        v_wg = {}
        v_wu = {}
        wait("gpsimd", s_pe, v_vmm_by[kv_iter - 1])
        v_wq[1] = dma("gpsimd", wqb[1].rearrange("p (c j) -> p c j", j=128),
                      wq_r[:, :, 128:256], s_wq[1])
        v_wq[0] = dma("gpsimd", wqb[0].rearrange("p (c j) -> p c j", j=128),
                      wq_r[:, :, 256:384], s_wq[0])
        v_wu[0] = dma("gpsimd", wub[0].rearrange("p (c j) -> p c j", j=128),
                      wuT_r[:, :, 0:128], s_kv[0])
        v_wu[1] = dma("gpsimd", wub[1].rearrange("p (c j) -> p c j", j=128),
                      wuT_r[:, :, 128:256], s_kv[1])
        wait("sync", s_kst, s_kst.v)
        wait("sync", s_vst, s_vst.v)
        wait("sync", s_dv, v_krope_by[kv_iter - 1])
        dma("sync", cq_t, cosq, s_ck)
        v_cq = dma("sync", sq_t, sinq, s_ck)
        v_kv = [0, 0]
        v_kv[0] = dma("sync", kv_k[0][:], kT_hbm[0, :, :], s_kk[0])
        v_kvv = dma("sync", kv_v[:].rearrange("p (j t) -> p j t", t=128),
                    v_hbm[0, :, :, :].rearrange("j p t -> p j t"), s_kvv)

        def ql_of(g, jp):
            return max(0, jp - 4 * g) * 128

        v_qproj_by = {}
        v_qrope_by = {}
        v_attn_done_by = {}
        a_exp_by = {}
        piter_box = [0]

        def q_proj(h):
            hb = h % 2
            if h == 0:
                wait("tensor", s_wd[0], v_wq0pre)
                lhs = wdb0_q
            else:
                wait("tensor", s_wq[hb], v_wq[hb])
                lhs = wqb[hb]
            if h >= 1:
                wait("tensor", s_dv, v_qrope_by[h - 1])
            fns = []
            for d in range(DC):
                for half in range(2):
                    fns.append(lambda eng, d=d, half=half, lhs=lhs: eng.matmul(
                        PS[:, half * 512:(half + 1) * 512],
                        lhs[:, d * 128:(d + 1) * 128],
                        xn_ownT[:, d * 1024 + half * 512: d * 1024 + (half + 1) * 512],
                        start=(d == 0), stop=(d == DC - 1)))
            v = group("tensor", fns, s_pe)
            v_qproj_by[h] = v
            return v

        def q_rope_start(h, v_qmm):
            hb = h % 2
            q1 = PS[0:64, 0:1024]
            q2 = PS[64:128, 0:1024]
            ops = [
                lambda eng: eng.tensor_mul(rq1, q1, cq_t),
                lambda eng: eng.tensor_mul(rq2, q2, sq_t),
                lambda eng, hb=hb: eng.tensor_sub(qbf[0:64, hb * 1024:(hb + 1) * 1024], rq1, rq2),
                lambda eng: eng.tensor_mul(rq1, q1, sq_t),
                lambda eng: eng.tensor_mul(rq2, q2, cq_t),
                lambda eng, hb=hb: eng.tensor_add(qbf[64:128, hb * 1024:(hb + 1) * 1024], rq1, rq2)]
            return {"i": 0, "ops": ops, "v": None, "vq": v_qmm, "h": h}

        def rope_step(st, n=1):
            if st is None:
                return
            for _ in range(n):
                i = st["i"]
                if i >= 6:
                    return
                if i == 0:
                    wait("vector", s_ck, v_cq)
                    wait("vector", s_pe, st["vq"])
                    if st["h"] >= 2 and (st["h"] - 2) in v_attn_done_by:
                        wait("vector", s_pe, v_attn_done_by[st["h"] - 2])
                else:
                    wait("vector", s_dv, st["v"])
                st["v"] = op("vector", st["ops"][i], s_dv)
                st["i"] += 1
                if st["i"] == 6:
                    v_qrope_by[st["h"]] = st["v"]

        v0 = q_proj(0)
        rope_step(q_rope_start(0, v0), 6)
        v_ctx_prev_head = None

        for h in range(NH):
            hb = h % 2
            kvh = h // 4
            kvb = kvh % 2
            if h % 4 == 0 and kvh + 1 < KVH:
                if kvh >= 1:
                    wait("sync", s_pe, v_attn_done_by[4 * kvh - 1])
                v_kv[(kvb + 1) % 2] = dma("sync", kv_k[(kvb + 1) % 2][:], kT_hbm[kvh + 1, :, :], s_kk[(kvb + 1) % 2])
            rs_next = None
            if h + 1 < NH:
                v_next = q_proj(h + 1)
                rs_next = q_rope_start(h + 1, v_next)
            if h + 3 < NH:
                wait("gpsimd", s_pe, v_qproj_by[h + 1])
                v_wq[(h + 3) % 2] = dma("gpsimd", wqb[(h + 3) % 2].rearrange("p (c j) -> p c j", j=128),
                                        wq_r[:, :, (h + 3) * 128:(h + 4) * 128], s_wq[(h + 3) % 2])
            elif h + 3 == NH:
                wait("gpsimd", s_pe, v_qproj_by[h + 1])
                v_wo[0] = dma("gpsimd", wob[0].rearrange("p (c j) -> p c j", j=128),
                              wo_r[:, :, 0:128], s_wq[0])

            wait("tensor", s_kk[kvb], v_kv[kvb])
            wait("tensor", s_kvv, v_kvv)
            wait("tensor", s_dv, v_qrope_by[h])
            v_ctx = v_ctx_prev_head
            v_grp = None
            for g in range(2):
                np_ = NPAIRS[g]
                pend = None

                def flush_pend(p):
                    if p is None:
                        return None
                    for wfn in p[0]:
                        wfn()
                    return group("tensor", p[1], s_pe)

                for jp in range(np_):
                    piter = piter_box[0]
                    ql = ql_of(g, jp)
                    stb = 2 + (piter % 2) * 2
                    if piter >= 2:
                        wait("tensor", s_ac, a_exp_by[piter - 2])
                    fns = []
                    for m in range(2):
                        j = 2 * jp + m
                        fns.append(lambda eng, j=j, m=m, ql=ql, g=g, hb=hb, kvb=kvb, stb=stb: eng.matmul(
                            PS[:, (stb + m) * 512 + ql: (stb + m + 1) * 512],
                            kv_k[kvb][:, j * 128:(j + 1) * 128],
                            qbf[:, hb * 1024 + g * 512 + ql: hb * 1024 + (g + 1) * 512],
                            start=True, stop=True))
                    v_st = group("tensor", fns, s_pe)
                    v_grp = flush_pend(pend)
                    wait("scalar", s_pe, v_st)

                    def fexp(eng, jp=jp, ql=ql, stb=stb):
                        return eng.activation(
                            PTr[:, 2 * jp:2 * jp + 2, ql:512],
                            PS[:, stb * 512:(stb + 2) * 512].rearrange("p (c q) -> p c q", q=512)[:, :, ql:512],
                            AF.Exp, scale=SCALE)
                    a_exp = op("scalar", fexp, s_ac)
                    a_exp_by[piter] = a_exp
                    pp = jp - 4 * g
                    is_masked = (g == 0) or (jp >= 4)
                    if is_masked:
                        wait("vector", s_ac, a_exp)

                        def fmask(eng, jp=jp, pp=pp):
                            return eng.tensor_mul(
                                PTr[:, 2 * jp:2 * jp + 2, pp * 128:(pp + 1) * 128],
                                PTr[:, 2 * jp:2 * jp + 2, pp * 128:(pp + 1) * 128],
                                mask_t[:].rearrange("p (c q) -> p c q", q=128))
                        v_mask = op("vector", fmask, s_dv)
                        if g == 0 and jp in (1, 3):
                            rope_step(rs_next, 1)
                        deps = [lambda vm=v_mask: wait("tensor", s_dv, vm)]
                    else:
                        deps = [lambda ae=a_exp: wait("tensor", s_ac, ae)]
                    if jp == 0 and v_ctx is not None:
                        deps.append(lambda vc=v_ctx: wait("tensor", s_dv, vc))
                    fns = []
                    for m in range(2):
                        j = 2 * jp + m
                        fns.append(lambda eng, j=j, ql=ql, g=g: eng.matmul(
                            PS[:, 6 * 512 + ql:7 * 512],
                            ones_t[:],
                            PT[:, j * 512 + ql:(j + 1) * 512],
                            start=(j == 0), stop=(j == 2 * NPAIRS[g] - 1)))
                        fns.append(lambda eng, j=j, ql=ql, g=g: eng.matmul(
                            PS[:, 7 * 512 + ql:8 * 512],
                            kv_v[:, j * 128:(j + 1) * 128],
                            PT[:, j * 512 + ql:(j + 1) * 512],
                            start=(j == 0), stop=(j == 2 * NPAIRS[g] - 1)))
                    pend = (deps, fns)
                    piter_box[0] += 1
                v_grp = flush_pend(pend)
                wait("vector", s_pe, v_grp)

                def fev(eng, h=h, g=g):
                    return eng.tensor_mul(
                        ctxT[:, h * 1024 + g * 512: h * 1024 + (g + 1) * 512],
                        bank(7), rbc)
                v_ctx = chain("vector", [
                    lambda eng: eng.reciprocal(rbc, bank(6)), fev], s_dv)
                if g == 0:
                    rope_step(rs_next, 4)
            v_attn_done_by[h] = v_grp
            rope_step(rs_next, 6)
            v_ctx_prev_head = v_ctx
            if h % 4 == 3 and kvh + 1 < KVH:
                wait("sync", s_pe, v_grp)
                v_kvv = dma("sync", kv_v[:].rearrange("p (j t) -> p j t", t=128),
                            v_hbm[kvh + 1, :, :, :].rearrange("j p t -> p j t"), s_kvv)

        v_ctxdbg = 0
        if debug:
            wait("sync", s_dv, s_dv.v)
            v_ctxdbg = dma("sync", ctx_dbg, ctxT[:], s_dbg)

        wait("tensor", s_dv, s_dv.v)
        wait("tensor", s_ac, s_ac.v)
        v_sdma_by = {}
        a_sev_by = {}
        v_smm_by = {}
        for hc in range(DC):
            hb = hc % 2
            if hc + 1 < DC:
                wait("gpsimd", s_pe, v_smm_by[hc - 1] if hc >= 1 else v_qproj_by[NH - 1])
                v_wo[(hc + 1) % 2] = dma("gpsimd", wob[(hc + 1) % 2].rearrange("p (c j) -> p c j", j=128),
                                         wo_r[:, :, (hc + 1) * 128:(hc + 2) * 128], s_wq[(hc + 1) % 2])
            bk0 = (hc % 2) * 2
            wait("tensor", s_wq[hb], v_wo[hb])
            if hc >= 2:
                wait("tensor", s_ac, a_sev_by[hc - 2])
            fns = []
            for d in range(DC):
                for half in range(2):
                    fns.append(lambda eng, d=d, half=half, hb=hb, bk0=bk0: eng.matmul(
                        PS[:, (bk0 + half) * 512: (bk0 + half + 1) * 512],
                        wob[hb][:, d * 128:(d + 1) * 128],
                        ctxT[:, d * 1024 + half * 512: d * 1024 + (half + 1) * 512],
                        start=(d == 0), stop=(d == DC - 1)))
            v_smm = group("tensor", fns, s_pe)
            v_smm_by[hc] = v_smm
            if hc == DC - 2:
                wait("gpsimd", s_pe, v_smm)
                v_wg[0] = dma("gpsimd", wgb[0].rearrange("p (c j) -> p c j", j=128),
                              wgT_r[:, :, 0:128], s_wq[0])
            wait("scalar", s_pe, v_smm)
            if hc >= 2:
                wait("scalar", s_ss[hb], v_sdma_by[hc - 2])
            a_sev = op("scalar", lambda eng, hb=hb, bk0=bk0: eng.activation(
                stile[:, hb * 1024:(hb + 1) * 1024], PS[:, bk0 * 512:(bk0 + 2) * 512], AF.Copy), s_ac)
            a_sev_by[hc] = a_sev
            wait("sync", s_ac, a_sev)
            v_sdma_by[hc] = dma("sync", sT_hbm[hc * 128:(hc + 1) * 128, :],
                                stile[:, hb * 1024:(hb + 1) * 1024], s_ss[hb])

        def prefetch_gu(i, bsel):
            v_wg[i] = dma("gpsimd", wgb[bsel].rearrange("p (c j) -> p c j", j=128),
                          wgT_r[:, :, i * 128:(i + 1) * 128], s_wq[bsel])
            v_wu[i] = dma("gpsimd", wub[bsel].rearrange("p (c j) -> p c j", j=128),
                          wuT_r[:, :, i * 128:(i + 1) * 128], s_kv[bsel])

        wait("gpsimd", s_pe, v_smm_by[DC - 1])
        wait("gpsimd", s_dv, s_dv.v)
        wait("sync", s_dv, s_dv.v)
        v_wg[1] = dma("gpsimd", wgb[1].rearrange("p (c j) -> p c j", j=128),
                      wgT_r[:, :, 128:256], s_wq[1])
        v_hT = {}
        a_g_by = {}
        v_u_by = {}
        wglob_box = [0]
        vwd_by = {}
        v_half_mms = {}
        eslot = [stile[:, 0:1024], stile[:, 1024:2048], gact[:, 0:1024], gact[:, 1024:2048]]
        pend_stores = []
        giter = 0
        v_dn_prev = None
        v_inj_prev = None
        eidx = 0
        v_spdma_by = {}

        v_fo_by = {}
        v_out_by = {}
        v_fs_by = {}
        a_ev_last_box = [0]
        deferred_pf = []

        def f_loads(hc):
            b = hc % 2
            if hc == 0:
                wait("gpsimd", s_ss[0], s_ss[0].v)
                wait("gpsimd", s_ss[1], s_ss[1].v)
                for _q in range(4):
                    wait("gpsimd", s_sp[_q], s_sp[_q].v)
            elif hc >= 2:
                wait("gpsimd", s_dv, v_fo_by[hc - 2])
                wait("gpsimd", s_o[b], v_out_by[hc - 2])
            dma("gpsimd", fxb2[b], x_ownT[hc * 128:(hc + 1) * 128, :], s_f[b])
            v_fs_by[hc] = dma("gpsimd", fsb2[b], sT_hbm[hc * 128:(hc + 1) * 128, :], s_f[b])

        def f_chunk(hc):
            b = hc % 2
            if hc == 0:
                wait("scalar", s_dv, v_hT[IC - 1])
                for _q in range(4):
                    wait("scalar", s_sp[_q], s_sp[_q].v)
            elif hc >= 2:
                wait("scalar", s_dv, v_fo_by[hc - 2])
            wait("scalar", s_ac, a_ev_last_box[0])
            a_si = op("scalar", lambda eng, hc=hc, b=b: eng.activation(
                fgb2[b], s_ffnT[:, hc * 1024:(hc + 1) * 1024], AF.Silu), s_ac)
            wait("vector", s_ac, a_si)
            wait("vector", s_f[b], v_fs_by[hc])
            v_fo_by[hc] = chain("vector", [
                lambda eng, b=b: eng.tensor_mul(fgb2[b], fsb2[b], fgb2[b]),
                lambda eng, b=b: eng.tensor_add(fxb2[b], fxb2[b], fgb2[b])], s_dv)
            wait("sync", s_dv, v_fo_by[hc])
            v_out_by[hc] = dma("sync", out_T[hc * 128:(hc + 1) * 128, :], fxb2[b], s_o[b])
            if hc + 2 < DC:
                f_loads(hc + 2)
        for qq in range(4):
            base, sz = QBASE[qq], QSZ[qq]
            for ii in range(sz):
                i = base + ii
                bsel = giter % 2
                gbk = (giter % 2) * 2
                ubk = 4 + (giter % 2) * 2
                wait("tensor", s_wq[bsel], v_wg[i])
                wait("tensor", s_kv[bsel], v_wu[i])
                if ii == 0:
                    wait("tensor", s_ac, s_ac.v)
                if giter >= 2 and ii >= 2:
                    wait("tensor", s_dv, v_hT[i - 2])
                    wait("tensor", s_ac, a_g_by[i - 2])
                elif giter >= 2 and ii == 1:
                    wait("tensor", s_dv, v_hT[i - 2])
                fns = []
                for d in range(DC):
                    for half in range(2):
                        fns.append(lambda eng, d=d, half=half, bsel=bsel, gbk=gbk: eng.matmul(
                            PS[:, (gbk + half) * 512:(gbk + half + 1) * 512],
                            wgb[bsel][:, d * 128:(d + 1) * 128],
                            xn_ownT[:, d * 1024 + half * 512: d * 1024 + (half + 1) * 512],
                            start=(d == 0), stop=(d == DC - 1)))
                v_g = group("tensor", fns, s_pe)
                fns = []
                for d in range(DC):
                    for half in range(2):
                        fns.append(lambda eng, d=d, half=half, bsel=bsel, ubk=ubk: eng.matmul(
                            PS[:, (ubk + half) * 512:(ubk + half + 1) * 512],
                            wub[bsel][:, d * 128:(d + 1) * 128],
                            xn_ownT[:, d * 1024 + half * 512: d * 1024 + (half + 1) * 512],
                            start=(d == 0), stop=(d == DC - 1)))
                v_u = group("tensor", fns, s_pe)
                v_u_by[i] = v_u
                if i + 2 < IC:
                    if ii >= sz - 2:
                        deferred_pf.append((i + 2, i % 2, v_u))
                    else:
                        wait("gpsimd", s_pe, v_u)
                        prefetch_gu(i + 2, i % 2)
                wait("scalar", s_pe, v_g)
                if giter >= 2:
                    wait("scalar", s_dv, v_hT[i - 2])
                if giter == 0:
                    wait("scalar", s_ss[0], v_sdma_by[DC - 2])
                    wait("scalar", s_ss[1], v_sdma_by[DC - 1])
                    wait("scalar", s_dv, s_dv.v)
                if ii == 0 and qq > 0:
                    for _q in range(4):
                        wait("scalar", s_sp[_q], s_sp[_q].v)
                gsl = giter % 2
                a_g = op("scalar", lambda eng, i=i, gbk=gbk, gsl=gsl: eng.activation(
                    gact[:, gsl * 1024:(gsl + 1) * 1024], PS[:, gbk * 512:(gbk + 2) * 512],
                    AF.Silu, scale=sgT_t[:, i:i + 1]), s_ac)
                wait("scalar", s_pe, v_u)
                a_us = op("scalar", lambda eng, i=i, ubk=ubk, gsl=gsl: eng.activation(
                    us_b[:, gsl * 1024:(gsl + 1) * 1024], PS[:, ubk * 512:(ubk + 2) * 512],
                    AF.Copy, scale=suT_t[:, i:i + 1]), s_ac)
                a_g_by[i] = a_us
                wait("vector", s_ac, a_us)
                if debug and giter == 0:
                    wait("vector", s_dbg, v_ctxdbg)

                def fh(eng, ii=ii, gsl=gsl):
                    return eng.tensor_mul(
                        hT[:, ii * 1024:(ii + 1) * 1024],
                        us_b[:, gsl * 1024:(gsl + 1) * 1024],
                        gact[:, gsl * 1024:(gsl + 1) * 1024])
                v_hT[i] = op("vector", fh, s_dv)
                giter += 1

            par = qq % 2
            nh0 = (sz + 1) // 2
            wbase = wglob_box[0]

            def wd_dma(w, base=base, sz=sz, nh0=nh0):
                hqw, hfw = w // 2, w % 2
                i0 = 0 if hfw == 0 else nh0
                i1 = nh0 if hfw == 0 else sz
                wg_ = wglob_box[0]
                slot = wg_ % 3
                if wg_ >= 3:
                    wait("gpsimd", s_pe, v_half_mms[wg_ - 3])
                else:
                    wait("gpsimd", s_pe, v_smm_by[DC - 1])
                vwd_by[wg_] = dma("gpsimd",
                                  wdb[slot][:, 0:(i1 - i0) * 512].rearrange("p (c j) -> p c j", j=512),
                                  wdT_r[:, base + i0:base + i1, hqw * 512:(hqw + 1) * 512], s_wd[slot])
                wglob_box[0] += 1
                return vwd_by[wg_]

            wd_dma(0)
            wd_dma(1)

            def spv_loads(hq):
                return [dma("sync", spv[:, l * 1024:(l + 1) * 1024],
                            sp_hbm[1 - par, (hq * 4 + l) * 128:(hq * 4 + l + 1) * 128, :], s_spv)
                        for l in range(4)]

            vsp_by = {}
            if qq > 0:
                for _q in range(4):
                    wait("sync", s_sp[_q], s_sp[_q].v)
                if v_inj_prev is not None:
                    wait("sync", s_pe, v_inj_prev)
                vsp_by[0] = spv_loads(0)
            if qq == 3:
                f_loads(0)
                f_loads(1)
            for hq in range(8):
                if hq == 0:
                    wait("tensor", s_dv, v_hT[base + sz - 1])
                if qq == 3 and hq >= 1:
                    wait("tensor", s_ac, a_ev_last_box[0])
                else:
                    wait("tensor", s_ac, s_ac.v)
                if qq > 0:
                    wait("tensor", s_spv, vsp_by[hq][3])
                    fns = []
                    for l in range(4):
                        for half in range(2):
                            fns.append(lambda eng, l=l, half=half: eng.matmul(
                                PS[:, (l * 2 + half) * 512:(l * 2 + half + 1) * 512],
                                iden_t[:],
                                spv[:, l * 1024 + half * 512: l * 1024 + (half + 1) * 512],
                                start=True, stop=False))
                    v_inj_prev = group("tensor", fns, s_pe)
                    if hq + 1 < 8:
                        wait("sync", s_pe, v_inj_prev)
                        vsp_by[hq + 1] = spv_loads(hq + 1)
                for hf in range(2):
                    i0 = 0 if hf == 0 else nh0
                    i1 = nh0 if hf == 0 else sz
                    w = hq * 2 + hf
                    wg_ = wbase + w
                    slot = wg_ % 3
                    wait("tensor", s_wd[slot], vwd_by[wg_])
                    fns = []
                    for ii in range(i0, i1):
                        for l in range(4):
                            for half in range(2):
                                fns.append(lambda eng, ii=ii, iii=(ii - i0), slot=slot, l=l, half=half, qq=qq,
                                           st=(qq == 0 and ii == 0), sp=(hf == 1 and ii == i1 - 1): eng.matmul(
                                    PS[:, (l * 2 + half) * 512:(l * 2 + half + 1) * 512],
                                    wdb[slot][:, iii * 512 + l * 128: iii * 512 + (l + 1) * 128],
                                    hT[:, ii * 1024 + half * 512: ii * 1024 + (half + 1) * 512],
                                    start=st, stop=sp))
                    v_dn_prev = group("tensor", fns, s_pe)
                    v_half_mms[wg_] = v_dn_prev
                    if w + 2 < 16:
                        wd_dma(w + 2)
                if deferred_pf and hq in (1, 2):
                    pi, pb, pvu = deferred_pf.pop(0)
                    wait("gpsimd", s_pe, pvu)
                    prefetch_gu(pi, pb)
                for (pa, pslot, phc, ppar) in pend_stores:
                    wait("sync", s_ac, pa)
                    v_spdma_by[eidx] = dma(
                        "sync", sp_hbm[ppar, phc * 128:(phc + 1) * 128, :],
                        eslot[pslot], s_sp[eidx % 4])
                    eidx += 1
                pend_stores = []
                wait("scalar", s_pe, v_dn_prev)
                for l in range(4):
                    hc = hq * 4 + l
                    if qq < 3:
                        if eidx + len(pend_stores) >= 4:
                            k = eidx + len(pend_stores) - 4
                            wait("scalar", s_sp[k % 4], v_spdma_by[k])
                        a_e = op("scalar", lambda eng, l=l: eng.activation(
                            eslot[l],
                            PS[:, l * 2 * 512:(l * 2 + 2) * 512], AF.Copy), s_ac)
                        pend_stores.append((a_e, l, hc, par))
                    else:
                        if debug and qq == 3 and hq == 0 and l == 0:
                            wait("scalar", s_dbg, s_dbg.v)
                        a_ev_last_box[0] = op("scalar", lambda eng, l=l, hc=hc: eng.activation(
                            s_ffnT[:, hc * 1024:(hc + 1) * 1024],
                            PS[:, l * 2 * 512:(l * 2 + 2) * 512], AF.Copy,
                            scale=sdT_t[:, hc:hc + 1]), s_ac)
                        f_chunk(hc)
            for (pa, pslot, phc, ppar) in pend_stores:
                wait("sync", s_ac, pa)
                v_spdma_by[eidx] = dma(
                    "sync", sp_hbm[ppar, phc * 128:(phc + 1) * 128, :],
                    eslot[pslot], s_sp[eidx % 4])
                eidx += 1
            pend_stores = []
        a_ffn_done = s_ac.v
        if debug:
            wait("sync", s_ac, a_ffn_done)
            dma("sync", sffn_dbg, s_ffnT[:], s_dbg)

        for _sm in (s_cst, s_ck, s_kst, s_vst, s_kk[0], s_kk[1], s_kvv, s_spv,
                    s_sp[0], s_sp[1], s_sp[2], s_sp[3], s_ss[0], s_ss[1], s_f[0], s_f[1],
                    s_f[2], s_f[3], s_o[0], s_o[1], s_o[2], s_o[3], s_dbg):
            wait("sync", _sm, _sm.v)
        for _sm in (s_x[0], s_x[1], s_kv[0], s_kv[1], s_wq[0], s_wq[1],
                    s_wd[0], s_wd[1], s_wd[2]):
            wait("gpsimd", _sm, _sm.v)

        @block.sync
        def _(eng):
            for f in prog["sync"]:
                f(eng)

        @block.gpsimd
        def _(eng):
            for f in prog["gpsimd"]:
                f(eng)

        @block.tensor
        def _(eng):
            for f in prog["tensor"]:
                f(eng)

        @block.vector
        def _(eng):
            for f in prog["vector"]:
                f(eng)

        @block.scalar
        def _(eng):
            for f in prog["scalar"]:
                f(eng)

    return nc


def make_core_inputs(d, core):
    bi, rho = core // 2, core % 2
    x = np.asarray(d["L0_attn_x"])
    NB = 16
    if rho == 0:
        order = []
        for p in range(NB // 2):
            order += [2 * p + 1, 2 * p]
    else:
        order = list(range(NB))
    xs = x[bi].reshape(NB, 128, D)[order].reshape(T, D)
    slot_pos = np.concatenate([np.arange(o * 128, o * 128 + 128) for o in order])
    own_idx = np.concatenate([np.arange(s * 128, s * 128 + 128) for s in range(1, NB, 2)])
    own_rows = slot_pos[own_idx]

    bf = ml_dtypes.bfloat16
    inv = (1.0 / 500000.0 ** (np.arange(0, HD, 2) / HD)).astype(np.float64)
    ang_k = slot_pos[:, None] * inv[None, :]
    cos_k = np.cos(ang_k).T.astype(bf)
    sin_k = np.sin(ang_k).T.astype(bf)
    ang_q = own_rows[:, None] * inv[None, :]
    cos_q = np.cos(ang_q).T.astype(bf)
    sin_q = np.sin(ang_q).T.astype(bf)

    m_even = (np.ones((128, 128)) if rho == 1 else np.zeros((128, 128))).astype(bf)
    m_diag = np.triu(np.ones((128, 128))).astype(bf)
    masks_np = np.concatenate([m_even, m_diag], axis=1)

    return {
        "x_seq": np.ascontiguousarray(xs, dtype=np.float32),
        "x_ownT": np.ascontiguousarray(x[bi][own_rows].T, dtype=np.float32),
        "cosq": np.ascontiguousarray(cos_q), "sinq": np.ascontiguousarray(sin_q),
        "cosk": np.ascontiguousarray(cos_k), "sink": np.ascontiguousarray(sin_k),
        "masks": np.ascontiguousarray(masks_np),
    }, own_rows, bi


def make_shared_inputs(d):
    bf = ml_dtypes.bfloat16
    return {
        "wq": np.ascontiguousarray(d["L0_attn_Wq"], dtype=np.float32),
        "wk": np.ascontiguousarray(d["L0_attn_Wk"], dtype=np.float32),
        "wv": np.ascontiguousarray(d["L0_attn_Wv"], dtype=np.float32),
        "wo": np.ascontiguousarray(d["L0_attn_Wo"], dtype=np.float32),
        "wgT": np.ascontiguousarray(np.asarray(d["L0_ffn_w_gate"]).T),
        "wuT": np.ascontiguousarray(np.asarray(d["L0_ffn_w_up"]).T),
        "wdT": np.ascontiguousarray(np.asarray(d["L0_ffn_w_down"]).T),
        "sgT": np.ascontiguousarray(np.asarray(d["L0_ffn_s_gate"], dtype=np.float32).reshape(IC, 128).T),
        "suT": np.ascontiguousarray(np.asarray(d["L0_ffn_s_up"], dtype=np.float32).reshape(IC, 128).T),
        "sdT": np.ascontiguousarray(np.asarray(d["L0_ffn_s_down"], dtype=np.float32).reshape(DC, 128).T),
        "iden": np.eye(128).astype(bf),
        "ones2": np.ones((128, 128)).astype(bf),
    }

from concourse.bass_utils import run_bass_kernel_spmd

_NC_CACHE = {}


def kernel(**inputs):
    shared = make_shared_inputs(inputs)
    if "nc" not in _NC_CACHE:
        _NC_CACHE["nc"] = build_nc(debug=False)
    nc = _NC_CACHE["nc"]
    in_maps = []
    metas = []
    for c in range(8):
        per, own_rows, bi = make_core_inputs(inputs, c)
        in_maps.append({**shared, **per})
        metas.append((bi, own_rows))
    res = run_bass_kernel_spmd(nc, in_maps, core_ids=list(range(8)))
    out = np.zeros((4, 2048, 4096), np.float32)
    for c, r in enumerate(res.results):
        bi, own_rows = metas[c]
        out[bi][own_rows] = np.asarray(r["out_T"]).T
    return out


# version 07
# speedup vs baseline: 1.2210x; 1.2210x over previous
import numpy as np
import ml_dtypes
import contextlib
import concourse.bass as bass
import concourse.mybir as mybir

F32 = mybir.dt.float32
BF16 = mybir.dt.bfloat16
I32 = mybir.dt.int32
AF = mybir.ActivationFunctionType
ALU = mybir.AluOpType
AX = mybir.AxisListType.X

D = 4096
T = 2048
NOWN = 1024
DC = 32
NH = 32
KVH = 8
HD = 128
NI = 11008
IC = 86
QSZ = [22, 22, 21, 21]
QBASE = [0, 22, 44, 65]
SCALE = float(1.0 / np.sqrt(HD))
EPS = 1e-5
NPAIRS = [4, 8]


class Sem:
    def __init__(self, h):
        self.h = h
        self.v = 0


def build_nc(debug=False):
    nc = bass.Bass("TRN2", target_bir_lowering=False)

    x_seq = nc.dram_tensor("x_seq", [T, D], F32, kind="ExternalInput").ap()
    x_ownT = nc.dram_tensor("x_ownT", [D, NOWN], F32, kind="ExternalInput").ap()
    wq = nc.dram_tensor("wq", [D, D], F32, kind="ExternalInput").ap()
    wk = nc.dram_tensor("wk", [D, KVH * HD], F32, kind="ExternalInput").ap()
    wv = nc.dram_tensor("wv", [D, KVH * HD], F32, kind="ExternalInput").ap()
    wo = nc.dram_tensor("wo", [D, D], F32, kind="ExternalInput").ap()
    wgT = nc.dram_tensor("wgT", [D, NI], I32, kind="ExternalInput").ap()
    wuT = nc.dram_tensor("wuT", [D, NI], I32, kind="ExternalInput").ap()
    wdT = nc.dram_tensor("wdT", [NI, D], I32, kind="ExternalInput").ap()
    sgT = nc.dram_tensor("sgT", [128, IC], F32, kind="ExternalInput").ap()
    suT = nc.dram_tensor("suT", [128, IC], F32, kind="ExternalInput").ap()
    sdT = nc.dram_tensor("sdT", [128, DC], F32, kind="ExternalInput").ap()
    cosq = nc.dram_tensor("cosq", [64, NOWN], BF16, kind="ExternalInput").ap()
    sinq = nc.dram_tensor("sinq", [64, NOWN], BF16, kind="ExternalInput").ap()
    cosk = nc.dram_tensor("cosk", [64, T], BF16, kind="ExternalInput").ap()
    sink = nc.dram_tensor("sink", [64, T], BF16, kind="ExternalInput").ap()
    masks = nc.dram_tensor("masks", [128, 256], BF16, kind="ExternalInput").ap()
    iden_in = nc.dram_tensor("iden", [128, 128], BF16, kind="ExternalInput").ap()
    ones_in = nc.dram_tensor("ones2", [128, 128], BF16, kind="ExternalInput").ap()

    out_T = nc.dram_tensor("out_T", [D, NOWN], F32, kind="ExternalOutput").ap()

    dbg = "ExternalOutput" if debug else "Internal"
    kT_hbm = nc.dram_tensor("kT_hbm", [KVH, 128, T], BF16, kind=dbg).ap()
    v_hbm = nc.dram_tensor("v_hbm", [KVH, 16, 128, 128], BF16, kind=dbg).ap()
    sT_hbm = nc.dram_tensor("sT_hbm", [D, NOWN], BF16, kind=dbg).ap()
    sp_hbm = nc.dram_tensor("sp_hbm", [2, D, NOWN], BF16, kind=dbg).ap()
    if debug:
        xn_dbg = nc.dram_tensor("xn_dbg", [128, DC * NOWN], BF16, kind="ExternalOutput").ap()
        ctx_dbg = nc.dram_tensor("ctx_dbg", [128, NH * NOWN], BF16, kind="ExternalOutput").ap()
        sffn_dbg = nc.dram_tensor("sffn_dbg", [128, DC * NOWN], BF16, kind="ExternalOutput").ap()

    with contextlib.ExitStack() as _stack:
        ec = _stack.enter_context
        R1 = ec(nc.sbuf_tensor("R1", [128, 32 * 1024], BF16))
        R2 = ec(nc.sbuf_tensor("R2", [128, 32 * 1024], BF16))
        R3 = ec(nc.sbuf_tensor("R3", [128, 22 * 1024], BF16))
        WB = ec(nc.sbuf_tensor("WB", [128, 16 * 1024], BF16))
        mask_t = ec(nc.sbuf_tensor("mask_t", [128, 256], BF16))
        iden_t = ec(nc.sbuf_tensor("iden_t", [128, 128], BF16))
        ones_t = ec(nc.sbuf_tensor("ones_t", [128, 128], BF16))
        sgT_t = ec(nc.sbuf_tensor("sgT_t", [128, IC], F32))
        suT_t = ec(nc.sbuf_tensor("suT_t", [128, IC], F32))
        sdT_t = ec(nc.sbuf_tensor("sdT_t", [128, DC], F32))
        stats = ec(nc.sbuf_tensor("stats", [128, 16], F32))
        PS = ec(nc.psum_tensor("PS", [128, 4096], F32))
        sem_names = ["s_pe", "s_ac", "s_dv", "s_cst", "s_x0", "s_x1",
                     "s_kv0", "s_kv1", "s_kst", "s_vst", "s_wq0", "s_wq1",
                     "s_kk0", "s_kk1", "s_kvv", "s_wd0", "s_wd1", "s_spv",
                     "s_sp0", "s_sp1", "s_ss0", "s_ss1", "s_f0", "s_f1",
                     "s_f2", "s_f3", "s_out", "s_ou1", "s_ou2", "s_ou3",
                     "s_wd2", "s_sp2", "s_sp3", "s_ck", "s_dbg"]
        SM = {n: Sem(ec(nc.semaphore(n))) for n in sem_names}
        block = ec(nc.Block())

        s_pe, s_ac, s_dv = SM["s_pe"], SM["s_ac"], SM["s_dv"]
        s_cst = SM["s_cst"]
        s_x = [SM["s_x0"], SM["s_x1"]]
        s_kv = [SM["s_kv0"], SM["s_kv1"]]
        s_kst, s_vst = SM["s_kst"], SM["s_vst"]
        s_wq = [SM["s_wq0"], SM["s_wq1"]]
        s_kk = [SM["s_kk0"], SM["s_kk1"]]
        s_kvv = SM["s_kvv"]
        s_wd = [SM["s_wd0"], SM["s_wd1"], SM["s_wd2"]]
        s_spv = SM["s_spv"]
        s_sp = [SM["s_sp0"], SM["s_sp1"], SM["s_sp2"], SM["s_sp3"]]
        s_ss = [SM["s_ss0"], SM["s_ss1"]]
        s_f = [SM["s_f0"], SM["s_f1"], SM["s_f2"], SM["s_f3"]]
        s_o = [SM["s_out"], SM["s_ou1"], SM["s_ou2"], SM["s_ou3"]]
        s_ck, s_dbg = SM["s_ck"], SM["s_dbg"]

        prog = {"sync": [], "gpsimd": [], "tensor": [], "vector": [], "scalar": []}
        qlast = {}

        def emit(eng, fn, tag=("o",)):
            prog[eng].append(fn)
            qlast[eng] = tag

        def dma(eng_name, out, in_, sem):
            if qlast.get(eng_name) != ("d", id(sem)) and sem.v > 0:
                emit(eng_name, lambda eng, val=sem.v: eng.wait_ge(sem.h, val))
            sem.v += 16
            v = sem.v
            emit(eng_name, lambda eng, out=out, in_=in_: eng.dma_start(out, in_).then_inc(sem.h, 16),
                 tag=("d", id(sem)))
            return v

        def wait(eng_name, sem, val):
            if val > 0:
                emit(eng_name, lambda eng, val=val: eng.wait_ge(sem.h, val))

        def op(eng_name, fn, sem=None):
            if sem is None:
                emit(eng_name, lambda eng, fn=fn: fn(eng))
                return None
            sem.v += 1
            v = sem.v
            emit(eng_name, lambda eng, fn=fn: fn(eng).then_inc(sem.h, 1))
            return v

        def chain(eng_name, fns, sem):
            v = None
            for fn in fns:
                wait(eng_name, sem, sem.v)
                v = op(eng_name, fn, sem)
            return v

        def group(eng_name, fns, sem):
            sem.v += 1
            v = sem.v
            for i, fn in enumerate(fns):
                if i == len(fns) - 1:
                    emit(eng_name, lambda eng, fn=fn: fn(eng).then_inc(sem.h, 1))
                else:
                    emit(eng_name, lambda eng, fn=fn: fn(eng))
            return v

        xn_ownT = R1
        s_ffnT = R1
        xn_qT = [R2[:, 0:DC * 512], R2[:, DC * 512:DC * 1024]]
        ctxT = R2
        hT = R2
        R3f = R3.bitcast(F32)
        xhalf = [[R3[:, 0:2048], R3[:, 2048:4096]],
                 [R3[:, 4096:6144], R3[:, 6144:8192]]]
        xntile2 = [R3[:, 8192:12288], R3[:, 15360:19456]]
        sqout = R3[:, 19456:20480]
        evk = R3[:, 12288:12800]
        evv = R3[:, 12800:13312]
        rk1 = R3[0:64, 13312:13824]
        rk2 = R3[0:64, 13824:14336]
        ck_s = R3[0:64, 14336:14848]
        sk_s = R3[0:64, 14848:15360]
        PT = R3[:, 0:8192]
        PTr = PT.rearrange("p (c q) -> p c q", q=512)
        kv_k = [R3[:, 8192:10240], R3[:, 10240:12288]]
        kv_v = R3[:, 12288:14336]
        qbf = R3[:, 14336:16384]
        cq_t = R3[0:64, 16384:17408]
        sq_t = R3[0:64, 17408:18432]
        rq1 = R3[0:64, 18432:19456]
        rq2 = R3[0:64, 19456:20480]
        rbc = R3f[:, 10240:10752]
        stile = R3[:, 19456:21504]
        wdb = [R3[:, 0:5632], R3[:, 5632:11264], R2[:, 22528:28160]]
        spv = R3[:, 11264:15360]
        gact = R3[:, 15360:17408]
        us_b = R3[:, 17408:19456]
        R2f = R2.bitcast(F32)
        fxb2 = [R2f[:, 14080 + i * 1024:14080 + (i + 1) * 1024] for i in range(2)]
        fsb2 = [R3[:, 19456 + i * 1024:19456 + (i + 1) * 1024] for i in range(2)]
        fgb2 = [R3f[:, 7680 + i * 1024:7680 + (i + 1) * 1024] for i in range(2)]

        def bank(i):
            return PS[:, i * 512:(i + 1) * 512]
        PSb = PS.bitcast(BF16)

        wk_r = wk.rearrange("(c p) h -> p c h", p=128)
        wv_r = wv.rearrange("(c p) h -> p c h", p=128)
        wq_r = wq.rearrange("(c p) h -> p c h", p=128)
        wo_r = wo.rearrange("(c p) h -> p c h", p=128)
        wgT_r = wgT.rearrange("(c p) i -> p c i", p=128)
        wuT_r = wuT.rearrange("(c p) i -> p c i", p=128)
        wdT_r = wdT.rearrange("(c p) h -> p c h", p=128)
        x_blocks = x_seq.rearrange("(b p) d -> b p d", p=128)

        wkb = [WB[:, 0:4096], WB[:, 4096:8192]]
        wvb = [WB[:, 8192:12288], WB[:, 12288:16384]]
        wqb = [WB[:, 0:4096], WB[:, 4096:8192]]
        wob = wqb
        wgb = [WB[:, 0:4096], WB[:, 4096:8192]]
        wub = [WB[:, 8192:12288], WB[:, 12288:16384]]

        for dst, src in ((mask_t[:], masks), (iden_t[:], iden_in), (ones_t[:], ones_in),
                         (sgT_t[:], sgT), (suT_t[:], suT), (sdT_t[:], sdT)):
            dma("sync", dst, src, s_cst)
        v_const = s_cst.v
        wait("tensor", s_cst, v_const)
        wait("scalar", s_cst, v_const)
        wait("vector", s_cst, v_const)

        a_norm_by = {}
        a_sq_by = {}
        v_tr_by = {}
        v_kmm_by = {}
        v_vmm_by = {}
        v_krope_by = {}
        a_vraw_by = {}
        evac_v_bank = {}
        A_state = {"kv_iter": 0, "v_kdma": None, "v_vt": None, "v_vev": None,
                   "v_vdma": None, "v_cksk": 0, "evac_v": None, "xn_ready": {}}

        def emit_tile(gb):
            qt = gb // 4
            tb = gb % 4
            xb = xhalf[gb % 2]
            xnt = xntile2[gb % 2]
            sx = s_x[gb % 2]
            sb8 = (gb % 2) * 8
            st1 = stats[:, sb8 + 0:sb8 + 1]
            st2 = stats[:, sb8 + 1:sb8 + 2]
            sq4 = [stats[:, sb8 + 2 + k:sb8 + 3 + k] for k in range(4)]
            mean_t = stats[:, sb8 + 6:sb8 + 7]
            rstd_t = stats[:, sb8 + 7:sb8 + 8]
            if gb >= 2:
                wait("gpsimd", s_dv, a_norm_by[gb - 2])
            dma("gpsimd", xb[0], x_blocks[gb, :, 0:2048], sx)
            vh1 = dma("gpsimd", xb[1], x_blocks[gb, :, 2048:4096], sx)

            wait("vector", sx, vh1)
            chain("vector", [
                lambda eng, xb=xb, st1=st1: eng.reduce_sum(st1, xb[0], axis=AX),
                lambda eng, xb=xb, st2=st2: eng.reduce_sum(st2, xb[1], axis=AX)], s_dv)
            wait("scalar", sx, vh1)
            if gb >= 1:
                wait("scalar", s_ac, a_sq_by[gb - 1])
            a_sq = chain("scalar", [
                (lambda eng, xb=xb, k=k, s4=sq4[k]: eng.activation(
                    sqout[:], xb[k // 2][:, (k % 2) * 1024:(k % 2 + 1) * 1024],
                    AF.Square, accum_out=s4)) for k in range(4)], s_ac)
            a_sq_by[gb] = a_sq
            wait("vector", s_ac, a_sq)
            v_var = chain("vector", [
                lambda eng, st1=st1, st2=st2: eng.tensor_add(st1, st1, st2),
                lambda eng, st1=st1, mean_t=mean_t: eng.tensor_scalar(mean_t, st1, 1.0 / D, None, ALU.mult),
                lambda eng, s4=sq4: eng.tensor_add(s4[0], s4[0], s4[1]),
                lambda eng, s4=sq4: eng.tensor_add(s4[2], s4[2], s4[3]),
                lambda eng, s4=sq4: eng.tensor_add(s4[0], s4[0], s4[2]),
                lambda eng, s4=sq4: eng.tensor_scalar(s4[0], s4[0], 1.0 / D, None, ALU.mult),
                lambda eng, st2=st2, mean_t=mean_t: eng.tensor_mul(st2, mean_t, mean_t),
                lambda eng, s4=sq4, st2=st2: eng.tensor_scalar(s4[0], s4[0], st2, float(EPS), ALU.subtract, ALU.add)], s_dv)
            wait("scalar", s_dv, v_var)
            a_rstd = chain("scalar", [
                lambda eng, s4=sq4: eng.activation(s4[1], s4[0], AF.Ln),
                lambda eng, s4=sq4, rstd_t=rstd_t: eng.activation(rstd_t, s4[1], AF.Exp, scale=-0.5)], s_ac)
            wait("vector", s_ac, a_rstd)
            if gb >= 2:
                wait("vector", s_pe, v_tr_by[gb - 2])
            a_norm = chain("vector", [
                lambda eng, xb=xb, xnt=xnt, mean_t=mean_t, rstd_t=rstd_t: eng.tensor_scalar(xnt[:, 0:2048], xb[0], mean_t, rstd_t, ALU.subtract, ALU.mult),
                lambda eng, xb=xb, xnt=xnt, mean_t=mean_t, rstd_t=rstd_t: eng.tensor_scalar(xnt[:, 2048:4096], xb[1], mean_t, rstd_t, ALU.subtract, ALU.mult)], s_dv)
            a_norm_by[gb] = a_norm

            wait("tensor", s_dv, a_norm)
            v_tr = None
            for cg in range(4):
                bk = cg % 2
                if bk in evac_v_bank:
                    wait("tensor", s_dv, evac_v_bank[bk])
                fns = []
                for cc in range(8):
                    c = cg * 8 + cc
                    fns.append(lambda eng, bk=bk, cc=cc, c=c, xnt=xnt: eng.transpose(
                        PSb[:, bk * 1024 + cc * 128: bk * 1024 + (cc + 1) * 128],
                        xnt[:, c * 128:(c + 1) * 128], iden_t[:]))
                v_tr = group("tensor", fns, s_pe)
                wait("vector", s_pe, v_tr)
                dst = xn_qT[qt % 2].rearrange("p (c t) -> p c t", t=512)[:, cg * 8:cg * 8 + 8, tb * 128:(tb + 1) * 128]
                src = PSb[:, bk * 1024: bk * 1024 + 1024].rearrange("p (c t) -> p c t", t=128)
                if gb % 2 == 1:
                    op("vector", lambda eng, dst=dst, src=src: eng.tensor_copy(dst, src))
                    dsto = xn_ownT.rearrange("p (c t) -> p c t", t=1024)[:, cg * 8:cg * 8 + 8, (gb // 2) * 128:(gb // 2) * 128 + 128]
                    evac_v = op("vector", lambda eng, dsto=dsto, src=src: eng.tensor_copy(dsto, src), s_dv)
                else:
                    evac_v = op("vector", lambda eng, dst=dst, src=src: eng.tensor_copy(dst, src), s_dv)
                evac_v_bank[bk] = evac_v
            v_tr_by[gb] = v_tr
            if tb == 3:
                A_state["xn_ready"][qt] = evac_v

        def emit_kv(qt, h):
            kv_iter = A_state["kv_iter"]
            bsel = kv_iter % 2
            skv = s_kv[bsel]
            if kv_iter >= 2:
                wait("gpsimd", s_pe, v_vmm_by[kv_iter - 2])
            dma("gpsimd", wkb[bsel].rearrange("p (c j) -> p c j", j=128),
                wk_r[:, :, h * 128:(h + 1) * 128], skv)
            vv_ = dma("gpsimd", wvb[bsel].rearrange("p (c j) -> p c j", j=128),
                      wv_r[:, :, h * 128:(h + 1) * 128], skv)
            kb = 2 + kv_iter % 2
            vb = 4 + kv_iter % 2
            wait("tensor", skv, vv_)
            wait("tensor", s_dv, A_state["xn_ready"][qt])
            if kv_iter >= 2:
                wait("tensor", s_dv, v_krope_by[kv_iter - 2])
            fns = [lambda eng, d=d, kb=kb, bsel=bsel, qt=qt: eng.matmul(
                bank(kb), wkb[bsel][:, d * 128:(d + 1) * 128],
                xn_qT[qt % 2][:, d * 512:(d + 1) * 512],
                start=(d == 0), stop=(d == DC - 1)) for d in range(DC)]
            v_kmm = group("tensor", fns, s_pe)
            v_kmm_by[kv_iter] = v_kmm
            wait("vector", s_pe, v_kmm)
            wait("vector", s_ck, A_state["v_cksk"])
            if A_state["v_kdma"] is not None:
                wait("vector", s_kst, A_state["v_kdma"])
            kq = bank(kb)
            v_krope = chain("vector", [
                lambda eng, kq=kq: eng.tensor_mul(rk1, kq[0:64, :], ck_s),
                lambda eng, kq=kq: eng.tensor_mul(rk2, kq[64:128, :], sk_s),
                lambda eng: eng.tensor_sub(evk[0:64, :], rk1, rk2),
                lambda eng, kq=kq: eng.tensor_mul(rk1, kq[0:64, :], sk_s),
                lambda eng, kq=kq: eng.tensor_mul(rk2, kq[64:128, :], ck_s),
                lambda eng: eng.tensor_add(evk[64:128, :], rk1, rk2)], s_dv)
            v_krope_by[kv_iter] = v_krope
            wait("sync", s_dv, v_krope)
            A_state["v_kdma"] = dma("sync", kT_hbm[h, :, qt * 512:(qt + 1) * 512], evk[:], s_kst)

            if kv_iter >= 2:
                wait("tensor", s_ac, a_vraw_by[kv_iter - 2])
            fns = [lambda eng, d=d, vb=vb, bsel=bsel, qt=qt: eng.matmul(
                bank(vb), wvb[bsel][:, d * 128:(d + 1) * 128],
                xn_qT[qt % 2][:, d * 512:(d + 1) * 512],
                start=(d == 0), stop=(d == DC - 1)) for d in range(DC)]
            v_vmm = group("tensor", fns, s_pe)
            v_vmm_by[kv_iter] = v_vmm
            wait("scalar", s_pe, v_vmm)
            if A_state["v_vt"] is not None:
                wait("scalar", s_pe, A_state["v_vt"])
            if A_state["v_vdma"] is not None:
                wait("scalar", s_vst, A_state["v_vdma"])
            a_vraw = op("scalar", lambda eng, vb=vb: eng.activation(evv[:], bank(vb), AF.Copy), s_ac)
            a_vraw_by[kv_iter] = a_vraw
            wait("tensor", s_ac, a_vraw)
            if A_state["v_vev"] is not None:
                wait("tensor", s_dv, A_state["v_vev"])
            fns = [lambda eng, j=j: eng.transpose(
                PSb[:, 6 * 1024 + j * 128:6 * 1024 + (j + 1) * 128],
                evv[:, j * 128:(j + 1) * 128], iden_t[:]) for j in range(4)]
            v_vt = group("tensor", fns, s_pe)
            A_state["v_vt"] = v_vt
            wait("vector", s_pe, v_vt)
            if A_state["v_vdma"] is not None:
                wait("vector", s_vst, A_state["v_vdma"])
            v_vev = op("vector", lambda eng: eng.tensor_copy(evv[:], PSb[:, 6 * 1024:6 * 1024 + 512]), s_dv)
            A_state["v_vev"] = v_vev
            wait("sync", s_dv, v_vev)
            A_state["v_vdma"] = dma("sync", v_hbm[h, qt * 4:(qt + 1) * 4, :, :].rearrange("j p t -> p j t"),
                                    evv[:].rearrange("p (j t) -> p j t", t=128), s_vst)
            A_state["kv_iter"] += 1

        dma("sync", ck_s, cosk[:, 0:512], s_ck)
        A_state["v_cksk"] = dma("sync", sk_s, sink[:, 0:512], s_ck)
        for tb in range(4):
            emit_tile(tb)
        for qt in range(4):
            if qt >= 1:
                wait("sync", s_dv, v_krope_by[A_state["kv_iter"] - 1])
                dma("sync", ck_s, cosk[:, qt * 512:(qt + 1) * 512], s_ck)
                A_state["v_cksk"] = dma("sync", sk_s, sink[:, qt * 512:(qt + 1) * 512], s_ck)
            for h in range(KVH):
                emit_kv(qt, h)
                if qt < 3 and h % 2 == 1:
                    emit_tile(4 * (qt + 1) + (h - 1) // 2)
        kv_iter = A_state["kv_iter"]
        wdb0_q = R3[:, 0:4096]
        wait("gpsimd", s_dv, a_norm_by[14])
        v_wq0pre = dma("gpsimd", wdb0_q.rearrange("p (c j) -> p c j", j=128),
                       wq_r[:, :, 0:128], s_wd[0])
        v_xndbg = 0
        if debug:
            wait("sync", s_dv, s_dv.v)
            v_xndbg = dma("sync", xn_dbg, xn_ownT[:], s_dbg)

        wait("tensor", s_dv, s_dv.v)
        wait("tensor", s_ac, s_ac.v)
        v_wq = [0, 0]
        v_wo = [0, 0]
        v_wg = {}
        v_wu = {}
        wait("gpsimd", s_pe, v_vmm_by[kv_iter - 1])
        v_wq[1] = dma("gpsimd", wqb[1].rearrange("p (c j) -> p c j", j=128),
                      wq_r[:, :, 128:256], s_wq[1])
        v_wq[0] = dma("gpsimd", wqb[0].rearrange("p (c j) -> p c j", j=128),
                      wq_r[:, :, 256:384], s_wq[0])
        v_wu[0] = dma("gpsimd", wub[0].rearrange("p (c j) -> p c j", j=128),
                      wuT_r[:, :, 0:128], s_kv[0])
        v_wu[1] = dma("gpsimd", wub[1].rearrange("p (c j) -> p c j", j=128),
                      wuT_r[:, :, 128:256], s_kv[1])
        wait("sync", s_kst, s_kst.v)
        wait("sync", s_vst, s_vst.v)
        wait("sync", s_dv, v_krope_by[kv_iter - 1])
        dma("sync", cq_t, cosq, s_ck)
        v_cq = dma("sync", sq_t, sinq, s_ck)
        v_kv = [0, 0]
        v_kv[0] = dma("sync", kv_k[0][:], kT_hbm[0, :, :], s_kk[0])
        v_kvv = dma("sync", kv_v[:].rearrange("p (j t) -> p j t", t=128),
                    v_hbm[0, :, :, :].rearrange("j p t -> p j t"), s_kvv)

        def ql_of(g, jp):
            return max(0, jp - 4 * g) * 128

        v_qproj_by = {}
        v_qrope_by = {}
        v_attn_done_by = {}
        a_exp_by = {}
        piter_box = [0]

        def q_proj(h):
            hb = h % 2
            if h == 0:
                wait("tensor", s_wd[0], v_wq0pre)
                lhs = wdb0_q
            else:
                wait("tensor", s_wq[hb], v_wq[hb])
                lhs = wqb[hb]
            if h >= 1:
                wait("tensor", s_dv, v_qrope_by[h - 1])
            fns = []
            for d in range(DC):
                for half in range(2):
                    fns.append(lambda eng, d=d, half=half, lhs=lhs: eng.matmul(
                        PS[:, half * 512:(half + 1) * 512],
                        lhs[:, d * 128:(d + 1) * 128],
                        xn_ownT[:, d * 1024 + half * 512: d * 1024 + (half + 1) * 512],
                        start=(d == 0), stop=(d == DC - 1)))
            v = group("tensor", fns, s_pe)
            v_qproj_by[h] = v
            return v

        def q_rope_start(h, v_qmm):
            hb = h % 2
            q1 = PS[0:64, 0:1024]
            q2 = PS[64:128, 0:1024]
            ops = [
                lambda eng: eng.tensor_mul(rq1, q1, cq_t),
                lambda eng: eng.tensor_mul(rq2, q2, sq_t),
                lambda eng, hb=hb: eng.tensor_sub(qbf[0:64, hb * 1024:(hb + 1) * 1024], rq1, rq2),
                lambda eng: eng.tensor_mul(rq1, q1, sq_t),
                lambda eng: eng.tensor_mul(rq2, q2, cq_t),
                lambda eng, hb=hb: eng.tensor_add(qbf[64:128, hb * 1024:(hb + 1) * 1024], rq1, rq2)]
            return {"i": 0, "ops": ops, "v": None, "vq": v_qmm, "h": h}

        def rope_step(st, n=1):
            if st is None:
                return
            for _ in range(n):
                i = st["i"]
                if i >= 6:
                    return
                if i == 0:
                    wait("vector", s_ck, v_cq)
                    wait("vector", s_pe, st["vq"])
                    if st["h"] >= 2 and (st["h"] - 2) in v_attn_done_by:
                        wait("vector", s_pe, v_attn_done_by[st["h"] - 2])
                else:
                    wait("vector", s_dv, st["v"])
                st["v"] = op("vector", st["ops"][i], s_dv)
                st["i"] += 1
                if st["i"] == 6:
                    v_qrope_by[st["h"]] = st["v"]

        v0 = q_proj(0)
        rope_step(q_rope_start(0, v0), 6)
        v_ctx_prev_head = None

        for h in range(NH):
            hb = h % 2
            kvh = h // 4
            kvb = kvh % 2
            if h % 4 == 0 and kvh + 1 < KVH:
                if kvh >= 1:
                    wait("sync", s_pe, v_attn_done_by[4 * kvh - 1])
                v_kv[(kvb + 1) % 2] = dma("sync", kv_k[(kvb + 1) % 2][:], kT_hbm[kvh + 1, :, :], s_kk[(kvb + 1) % 2])
            rs_next = None
            if h + 1 < NH:
                v_next = q_proj(h + 1)
                rs_next = q_rope_start(h + 1, v_next)
            if h + 3 < NH:
                wait("gpsimd", s_pe, v_qproj_by[h + 1])
                v_wq[(h + 3) % 2] = dma("gpsimd", wqb[(h + 3) % 2].rearrange("p (c j) -> p c j", j=128),
                                        wq_r[:, :, (h + 3) * 128:(h + 4) * 128], s_wq[(h + 3) % 2])
            elif h + 3 == NH:
                wait("gpsimd", s_pe, v_qproj_by[h + 1])
                v_wo[0] = dma("gpsimd", wob[0].rearrange("p (c j) -> p c j", j=128),
                              wo_r[:, :, 0:128], s_wq[0])

            wait("tensor", s_kk[kvb], v_kv[kvb])
            wait("tensor", s_kvv, v_kvv)
            wait("tensor", s_dv, v_qrope_by[h])
            v_ctx = v_ctx_prev_head
            v_grp = None
            for g in range(2):
                np_ = NPAIRS[g]
                pend_q = []

                def flush_pend(p):
                    if p is None:
                        return None
                    for wfn in p[0]:
                        wfn()
                    return group("tensor", p[1], s_pe)

                for jp in range(np_):
                    piter = piter_box[0]
                    ql = ql_of(g, jp)
                    stb = 2 + (piter % 2) * 2
                    if piter >= 2:
                        wait("tensor", s_ac, a_exp_by[piter - 2])
                    fns = []
                    for m in range(2):
                        j = 2 * jp + m
                        fns.append(lambda eng, j=j, m=m, ql=ql, g=g, hb=hb, kvb=kvb, stb=stb: eng.matmul(
                            PS[:, (stb + m) * 512 + ql: (stb + m + 1) * 512],
                            kv_k[kvb][:, j * 128:(j + 1) * 128],
                            qbf[:, hb * 1024 + g * 512 + ql: hb * 1024 + (g + 1) * 512],
                            start=True, stop=True))
                    v_st = group("tensor", fns, s_pe)
                    if len(pend_q) == 2:
                        v_grp = flush_pend(pend_q.pop(0))
                    wait("scalar", s_pe, v_st)

                    def fexp(eng, jp=jp, ql=ql, stb=stb):
                        return eng.activation(
                            PTr[:, 2 * jp:2 * jp + 2, ql:512],
                            PS[:, stb * 512:(stb + 2) * 512].rearrange("p (c q) -> p c q", q=512)[:, :, ql:512],
                            AF.Exp, scale=SCALE)
                    a_exp = op("scalar", fexp, s_ac)
                    a_exp_by[piter] = a_exp
                    pp = jp - 4 * g
                    is_masked = (g == 0) or (jp >= 4)
                    if is_masked:
                        wait("vector", s_ac, a_exp)

                        def fmask(eng, jp=jp, pp=pp):
                            return eng.tensor_mul(
                                PTr[:, 2 * jp:2 * jp + 2, pp * 128:(pp + 1) * 128],
                                PTr[:, 2 * jp:2 * jp + 2, pp * 128:(pp + 1) * 128],
                                mask_t[:].rearrange("p (c q) -> p c q", q=128))
                        v_mask = op("vector", fmask, s_dv)
                        if (g == 0 and jp in (1, 3)) or (g == 1 and jp in (4, 5)):
                            rope_step(rs_next, 1)
                        deps = [lambda vm=v_mask: wait("tensor", s_dv, vm)]
                    else:
                        deps = [lambda ae=a_exp: wait("tensor", s_ac, ae)]
                    if jp == 0 and v_ctx is not None:
                        deps.append(lambda vc=v_ctx: wait("tensor", s_dv, vc))
                    fns = []
                    for m in range(2):
                        j = 2 * jp + m
                        fns.append(lambda eng, j=j, ql=ql, g=g: eng.matmul(
                            PS[:, 6 * 512 + ql:7 * 512],
                            ones_t[:],
                            PT[:, j * 512 + ql:(j + 1) * 512],
                            start=(j == 0), stop=(j == 2 * NPAIRS[g] - 1)))
                        fns.append(lambda eng, j=j, ql=ql, g=g: eng.matmul(
                            PS[:, 7 * 512 + ql:8 * 512],
                            kv_v[:, j * 128:(j + 1) * 128],
                            PT[:, j * 512 + ql:(j + 1) * 512],
                            start=(j == 0), stop=(j == 2 * NPAIRS[g] - 1)))
                    pend_q.append((deps, fns))
                    piter_box[0] += 1
                for _p in pend_q:
                    v_grp = flush_pend(_p)
                pend_q = []
                wait("vector", s_pe, v_grp)

                def fev(eng, h=h, g=g):
                    return eng.tensor_mul(
                        ctxT[:, h * 1024 + g * 512: h * 1024 + (g + 1) * 512],
                        bank(7), rbc)
                v_ctx = chain("vector", [
                    lambda eng: eng.reciprocal(rbc, bank(6)), fev], s_dv)
                if g == 0:
                    rope_step(rs_next, 2)
            v_attn_done_by[h] = v_grp
            rope_step(rs_next, 6)
            v_ctx_prev_head = v_ctx
            if h % 4 == 3 and kvh + 1 < KVH:
                wait("sync", s_pe, v_grp)
                v_kvv = dma("sync", kv_v[:].rearrange("p (j t) -> p j t", t=128),
                            v_hbm[kvh + 1, :, :, :].rearrange("j p t -> p j t"), s_kvv)

        v_ctxdbg = 0
        if debug:
            wait("sync", s_dv, s_dv.v)
            v_ctxdbg = dma("sync", ctx_dbg, ctxT[:], s_dbg)

        wait("tensor", s_dv, s_dv.v)
        wait("tensor", s_ac, s_ac.v)
        v_sdma_by = {}
        a_sev_by = {}
        v_smm_by = {}
        for hc in range(DC):
            hb = hc % 2
            if hc + 1 < DC:
                wait("gpsimd", s_pe, v_smm_by[hc - 1] if hc >= 1 else v_qproj_by[NH - 1])
                v_wo[(hc + 1) % 2] = dma("gpsimd", wob[(hc + 1) % 2].rearrange("p (c j) -> p c j", j=128),
                                         wo_r[:, :, (hc + 1) * 128:(hc + 2) * 128], s_wq[(hc + 1) % 2])
            bk0 = (hc % 2) * 2
            wait("tensor", s_wq[hb], v_wo[hb])
            if hc >= 2:
                wait("tensor", s_ac, a_sev_by[hc - 2])
            fns = []
            for d in range(DC):
                for half in range(2):
                    fns.append(lambda eng, d=d, half=half, hb=hb, bk0=bk0: eng.matmul(
                        PS[:, (bk0 + half) * 512: (bk0 + half + 1) * 512],
                        wob[hb][:, d * 128:(d + 1) * 128],
                        ctxT[:, d * 1024 + half * 512: d * 1024 + (half + 1) * 512],
                        start=(d == 0), stop=(d == DC - 1)))
            v_smm = group("tensor", fns, s_pe)
            v_smm_by[hc] = v_smm
            if hc == DC - 2:
                wait("gpsimd", s_pe, v_smm)
                v_wg[0] = dma("gpsimd", wgb[0].rearrange("p (c j) -> p c j", j=128),
                              wgT_r[:, :, 0:128], s_wq[0])
            wait("scalar", s_pe, v_smm)
            if hc >= 2:
                wait("scalar", s_ss[hb], v_sdma_by[hc - 2])
            a_sev = op("scalar", lambda eng, hb=hb, bk0=bk0: eng.activation(
                stile[:, hb * 1024:(hb + 1) * 1024], PS[:, bk0 * 512:(bk0 + 2) * 512], AF.Copy), s_ac)
            a_sev_by[hc] = a_sev
            wait("sync", s_ac, a_sev)
            v_sdma_by[hc] = dma("sync", sT_hbm[hc * 128:(hc + 1) * 128, :],
                                stile[:, hb * 1024:(hb + 1) * 1024], s_ss[hb])

        def prefetch_gu(i, bsel):
            v_wg[i] = dma("gpsimd", wgb[bsel].rearrange("p (c j) -> p c j", j=128),
                          wgT_r[:, :, i * 128:(i + 1) * 128], s_wq[bsel])
            v_wu[i] = dma("gpsimd", wub[bsel].rearrange("p (c j) -> p c j", j=128),
                          wuT_r[:, :, i * 128:(i + 1) * 128], s_kv[bsel])

        wait("gpsimd", s_pe, v_smm_by[DC - 1])
        wait("gpsimd", s_dv, s_dv.v)
        wait("sync", s_dv, s_dv.v)
        v_wg[1] = dma("gpsimd", wgb[1].rearrange("p (c j) -> p c j", j=128),
                      wgT_r[:, :, 128:256], s_wq[1])
        v_hT = {}
        a_g_by = {}
        v_u_by = {}
        wglob_box = [0]
        vwd_by = {}
        v_half_mms = {}
        eslot = [stile[:, 0:1024], stile[:, 1024:2048], gact[:, 0:1024], gact[:, 1024:2048]]
        pend_stores = []
        giter = 0
        v_dn_prev = None
        v_inj_prev = None
        eidx = 0
        v_spdma_by = {}

        v_fo_by = {}
        v_out_by = {}
        v_fs_by = {}
        ev_ac_box = [0]
        ev_dv_box = [0]
        ev_gate = {}
        deferred_pf = []

        def f_loads(hc):
            b = hc % 2
            if hc == 0:
                wait("gpsimd", s_ss[0], s_ss[0].v)
                wait("gpsimd", s_ss[1], s_ss[1].v)
                for _q in range(4):
                    wait("gpsimd", s_sp[_q], s_sp[_q].v)
            elif hc >= 2:
                wait("gpsimd", s_dv, v_fo_by[hc - 2])
                wait("gpsimd", s_o[b], v_out_by[hc - 2])
            dma("gpsimd", fxb2[b], x_ownT[hc * 128:(hc + 1) * 128, :], s_f[b])
            v_fs_by[hc] = dma("gpsimd", fsb2[b], sT_hbm[hc * 128:(hc + 1) * 128, :], s_f[b])

        def f_chunk(hc):
            b = hc % 2
            if hc == 0:
                wait("scalar", s_dv, v_hT[IC - 1])
                for _q in range(4):
                    wait("scalar", s_sp[_q], s_sp[_q].v)
            elif hc >= 2:
                wait("scalar", s_dv, v_fo_by[hc - 2])
            esem, ev = ev_gate[hc]
            wait("scalar", esem, ev)
            a_si = op("scalar", lambda eng, hc=hc, b=b: eng.activation(
                fgb2[b], s_ffnT[:, hc * 1024:(hc + 1) * 1024], AF.Silu), s_ac)
            wait("vector", s_ac, a_si)
            wait("vector", s_f[b], v_fs_by[hc])
            v_fo_by[hc] = chain("vector", [
                lambda eng, b=b: eng.tensor_mul(fgb2[b], fsb2[b], fgb2[b]),
                lambda eng, b=b: eng.tensor_add(fxb2[b], fxb2[b], fgb2[b])], s_dv)
            wait("sync", s_dv, v_fo_by[hc])
            v_out_by[hc] = dma("sync", out_T[hc * 128:(hc + 1) * 128, :], fxb2[b], s_o[b])
            if hc + 2 < DC:
                f_loads(hc + 2)
        for qq in range(4):
            base, sz = QBASE[qq], QSZ[qq]
            for ii in range(sz):
                i = base + ii
                bsel = giter % 2
                gbk = (giter % 2) * 2
                ubk = 4 + (giter % 2) * 2
                wait("tensor", s_wq[bsel], v_wg[i])
                wait("tensor", s_kv[bsel], v_wu[i])
                if ii == 0:
                    wait("tensor", s_ac, s_ac.v)
                    wait("tensor", s_dv, s_dv.v)
                if giter >= 2 and ii >= 2:
                    wait("tensor", s_dv, v_hT[i - 2])
                    wait("tensor", s_ac, a_g_by[i - 2])
                elif giter >= 2 and ii == 1:
                    wait("tensor", s_dv, v_hT[i - 2])
                fns = []
                for d in range(DC):
                    for half in range(2):
                        fns.append(lambda eng, d=d, half=half, bsel=bsel, gbk=gbk: eng.matmul(
                            PS[:, (gbk + half) * 512:(gbk + half + 1) * 512],
                            wgb[bsel][:, d * 128:(d + 1) * 128],
                            xn_ownT[:, d * 1024 + half * 512: d * 1024 + (half + 1) * 512],
                            start=(d == 0), stop=(d == DC - 1)))
                v_g = group("tensor", fns, s_pe)
                fns = []
                for d in range(DC):
                    for half in range(2):
                        fns.append(lambda eng, d=d, half=half, bsel=bsel, ubk=ubk: eng.matmul(
                            PS[:, (ubk + half) * 512:(ubk + half + 1) * 512],
                            wub[bsel][:, d * 128:(d + 1) * 128],
                            xn_ownT[:, d * 1024 + half * 512: d * 1024 + (half + 1) * 512],
                            start=(d == 0), stop=(d == DC - 1)))
                v_u = group("tensor", fns, s_pe)
                v_u_by[i] = v_u
                if i + 2 < IC:
                    if ii >= sz - 2:
                        deferred_pf.append((i + 2, i % 2, v_u))
                    else:
                        wait("gpsimd", s_pe, v_u)
                        prefetch_gu(i + 2, i % 2)
                wait("scalar", s_pe, v_g)
                if giter >= 2:
                    wait("scalar", s_dv, v_hT[i - 2])
                if giter == 0:
                    wait("scalar", s_ss[0], v_sdma_by[DC - 2])
                    wait("scalar", s_ss[1], v_sdma_by[DC - 1])
                    wait("scalar", s_dv, s_dv.v)
                if ii == 0 and qq > 0:
                    for _q in range(4):
                        wait("scalar", s_sp[_q], s_sp[_q].v)
                gsl = giter % 2
                a_g = op("scalar", lambda eng, i=i, gbk=gbk, gsl=gsl: eng.activation(
                    gact[:, gsl * 1024:(gsl + 1) * 1024], PS[:, gbk * 512:(gbk + 2) * 512],
                    AF.Silu, scale=sgT_t[:, i:i + 1]), s_ac)
                wait("scalar", s_pe, v_u)
                a_us = op("scalar", lambda eng, i=i, ubk=ubk, gsl=gsl: eng.activation(
                    us_b[:, gsl * 1024:(gsl + 1) * 1024], PS[:, ubk * 512:(ubk + 2) * 512],
                    AF.Copy, scale=suT_t[:, i:i + 1]), s_ac)
                a_g_by[i] = a_us
                wait("vector", s_ac, a_us)
                if debug and giter == 0:
                    wait("vector", s_dbg, v_ctxdbg)

                def fh(eng, ii=ii, gsl=gsl):
                    return eng.tensor_mul(
                        hT[:, ii * 1024:(ii + 1) * 1024],
                        us_b[:, gsl * 1024:(gsl + 1) * 1024],
                        gact[:, gsl * 1024:(gsl + 1) * 1024])
                v_hT[i] = op("vector", fh, s_dv)
                giter += 1

            par = qq % 2
            nh0 = (sz + 1) // 2
            wbase = wglob_box[0]

            def wd_dma(w, base=base, sz=sz, nh0=nh0):
                hqw, hfw = w // 2, w % 2
                i0 = 0 if hfw == 0 else nh0
                i1 = nh0 if hfw == 0 else sz
                wg_ = wglob_box[0]
                slot = wg_ % 3
                if wg_ >= 3:
                    wait("gpsimd", s_pe, v_half_mms[wg_ - 3])
                else:
                    wait("gpsimd", s_pe, v_smm_by[DC - 1])
                vwd_by[wg_] = dma("gpsimd",
                                  wdb[slot][:, 0:(i1 - i0) * 512].rearrange("p (c j) -> p c j", j=512),
                                  wdT_r[:, base + i0:base + i1, hqw * 512:(hqw + 1) * 512], s_wd[slot])
                wglob_box[0] += 1
                return vwd_by[wg_]

            wd_dma(0)
            wd_dma(1)

            def spv_loads(hq):
                return [dma("sync", spv[:, l * 1024:(l + 1) * 1024],
                            sp_hbm[1 - par, (hq * 4 + l) * 128:(hq * 4 + l + 1) * 128, :], s_spv)
                        for l in range(4)]

            vsp_by = {}
            if qq > 0:
                for _q in range(4):
                    wait("sync", s_sp[_q], s_sp[_q].v)
                if v_inj_prev is not None:
                    wait("sync", s_pe, v_inj_prev)
                vsp_by[0] = spv_loads(0)
            if qq == 3:
                f_loads(0)
                f_loads(1)
            for hq in range(8):
                if hq == 0:
                    wait("tensor", s_dv, v_hT[base + sz - 1])
                if qq == 3 and hq >= 1:
                    wait("tensor", s_ac, ev_ac_box[0])
                    wait("tensor", s_dv, ev_dv_box[0])
                else:
                    wait("tensor", s_ac, s_ac.v)
                    wait("tensor", s_dv, s_dv.v)
                if qq > 0:
                    wait("tensor", s_spv, vsp_by[hq][3])
                    fns = []
                    for l in range(4):
                        for half in range(2):
                            fns.append(lambda eng, l=l, half=half: eng.matmul(
                                PS[:, (l * 2 + half) * 512:(l * 2 + half + 1) * 512],
                                iden_t[:],
                                spv[:, l * 1024 + half * 512: l * 1024 + (half + 1) * 512],
                                start=True, stop=False))
                    v_inj_prev = group("tensor", fns, s_pe)
                    if hq + 1 < 8:
                        wait("sync", s_pe, v_inj_prev)
                        vsp_by[hq + 1] = spv_loads(hq + 1)
                for hf in range(2):
                    i0 = 0 if hf == 0 else nh0
                    i1 = nh0 if hf == 0 else sz
                    w = hq * 2 + hf
                    wg_ = wbase + w
                    slot = wg_ % 3
                    wait("tensor", s_wd[slot], vwd_by[wg_])
                    fns = []
                    for ii in range(i0, i1):
                        for l in range(4):
                            for half in range(2):
                                fns.append(lambda eng, ii=ii, iii=(ii - i0), slot=slot, l=l, half=half, qq=qq,
                                           st=(qq == 0 and ii == 0), sp=(hf == 1 and ii == i1 - 1): eng.matmul(
                                    PS[:, (l * 2 + half) * 512:(l * 2 + half + 1) * 512],
                                    wdb[slot][:, iii * 512 + l * 128: iii * 512 + (l + 1) * 128],
                                    hT[:, ii * 1024 + half * 512: ii * 1024 + (half + 1) * 512],
                                    start=st, stop=sp))
                    v_dn_prev = group("tensor", fns, s_pe)
                    v_half_mms[wg_] = v_dn_prev
                    if w + 2 < 16:
                        wd_dma(w + 2)
                if deferred_pf and hq in (1, 2):
                    pi, pb, pvu = deferred_pf.pop(0)
                    wait("gpsimd", s_pe, pvu)
                    prefetch_gu(pi, pb)
                for (psem, pv, pslot, phc, ppar) in pend_stores:
                    wait("sync", psem, pv)
                    v_spdma_by[eidx] = dma(
                        "sync", sp_hbm[ppar, phc * 128:(phc + 1) * 128, :],
                        eslot[pslot], s_sp[eidx % 4])
                    eidx += 1
                pend_stores = []
                wait("scalar", s_pe, v_dn_prev)
                wait("vector", s_pe, v_dn_prev)
                for l in range(4):
                    hc = hq * 4 + l
                    edev = "scalar" if l < 2 else "vector"
                    esem = s_ac if l < 2 else s_dv
                    if qq < 3:
                        if eidx + len(pend_stores) >= 4:
                            k = eidx + len(pend_stores) - 4
                            wait(edev, s_sp[k % 4], v_spdma_by[k])
                        if l < 2:
                            a_e = op("scalar", lambda eng, l=l: eng.activation(
                                eslot[l],
                                PS[:, l * 2 * 512:(l * 2 + 2) * 512], AF.Copy), s_ac)
                        else:
                            a_e = op("vector", lambda eng, l=l: eng.tensor_copy(
                                eslot[l],
                                PS[:, l * 2 * 512:(l * 2 + 2) * 512]), s_dv)
                        pend_stores.append((esem, a_e, l, hc, par))
                    else:
                        if debug and qq == 3 and hq == 0 and l == 0:
                            wait("scalar", s_dbg, s_dbg.v)
                        if l < 2:
                            ev = op("scalar", lambda eng, l=l, hc=hc: eng.activation(
                                s_ffnT[:, hc * 1024:(hc + 1) * 1024],
                                PS[:, l * 2 * 512:(l * 2 + 2) * 512], AF.Copy,
                                scale=sdT_t[:, hc:hc + 1]), s_ac)
                            ev_ac_box[0] = ev
                        else:
                            ev = op("vector", lambda eng, l=l, hc=hc: eng.tensor_scalar(
                                s_ffnT[:, hc * 1024:(hc + 1) * 1024],
                                PS[:, l * 2 * 512:(l * 2 + 2) * 512],
                                sdT_t[:, hc:hc + 1], None, ALU.mult), s_dv)
                            ev_dv_box[0] = ev
                        ev_gate[hc] = (esem, ev)
                if qq == 3:
                    for l in range(4):
                        f_chunk(hq * 4 + l)
            for (psem, pv, pslot, phc, ppar) in pend_stores:
                wait("sync", psem, pv)
                v_spdma_by[eidx] = dma(
                    "sync", sp_hbm[ppar, phc * 128:(phc + 1) * 128, :],
                    eslot[pslot], s_sp[eidx % 4])
                eidx += 1
            pend_stores = []
        a_ffn_done = s_ac.v
        if debug:
            wait("sync", s_ac, a_ffn_done)
            dma("sync", sffn_dbg, s_ffnT[:], s_dbg)

        for _sm in (s_cst, s_ck, s_kst, s_vst, s_kk[0], s_kk[1], s_kvv, s_spv,
                    s_sp[0], s_sp[1], s_sp[2], s_sp[3], s_ss[0], s_ss[1], s_f[0], s_f[1],
                    s_f[2], s_f[3], s_o[0], s_o[1], s_o[2], s_o[3], s_dbg):
            wait("sync", _sm, _sm.v)
        for _sm in (s_x[0], s_x[1], s_kv[0], s_kv[1], s_wq[0], s_wq[1],
                    s_wd[0], s_wd[1], s_wd[2]):
            wait("gpsimd", _sm, _sm.v)

        @block.sync
        def _(eng):
            for f in prog["sync"]:
                f(eng)

        @block.gpsimd
        def _(eng):
            for f in prog["gpsimd"]:
                f(eng)

        @block.tensor
        def _(eng):
            for f in prog["tensor"]:
                f(eng)

        @block.vector
        def _(eng):
            for f in prog["vector"]:
                f(eng)

        @block.scalar
        def _(eng):
            for f in prog["scalar"]:
                f(eng)

    return nc


def make_core_inputs(d, core):
    bi, rho = core // 2, core % 2
    x = np.asarray(d["L0_attn_x"])
    NB = 16
    if rho == 0:
        order = []
        for p in range(NB // 2):
            order += [2 * p + 1, 2 * p]
    else:
        order = list(range(NB))
    xs = x[bi].reshape(NB, 128, D)[order].reshape(T, D)
    slot_pos = np.concatenate([np.arange(o * 128, o * 128 + 128) for o in order])
    own_idx = np.concatenate([np.arange(s * 128, s * 128 + 128) for s in range(1, NB, 2)])
    own_rows = slot_pos[own_idx]

    bf = ml_dtypes.bfloat16
    inv = (1.0 / 500000.0 ** (np.arange(0, HD, 2) / HD)).astype(np.float64)
    ang_k = slot_pos[:, None] * inv[None, :]
    cos_k = np.cos(ang_k).T.astype(bf)
    sin_k = np.sin(ang_k).T.astype(bf)
    ang_q = own_rows[:, None] * inv[None, :]
    cos_q = np.cos(ang_q).T.astype(bf)
    sin_q = np.sin(ang_q).T.astype(bf)

    m_even = (np.ones((128, 128)) if rho == 1 else np.zeros((128, 128))).astype(bf)
    m_diag = np.triu(np.ones((128, 128))).astype(bf)
    masks_np = np.concatenate([m_even, m_diag], axis=1)

    return {
        "x_seq": np.ascontiguousarray(xs, dtype=np.float32),
        "x_ownT": np.ascontiguousarray(x[bi][own_rows].T, dtype=np.float32),
        "cosq": np.ascontiguousarray(cos_q), "sinq": np.ascontiguousarray(sin_q),
        "cosk": np.ascontiguousarray(cos_k), "sink": np.ascontiguousarray(sin_k),
        "masks": np.ascontiguousarray(masks_np),
    }, own_rows, bi


def make_shared_inputs(d):
    bf = ml_dtypes.bfloat16
    return {
        "wq": np.ascontiguousarray(d["L0_attn_Wq"], dtype=np.float32),
        "wk": np.ascontiguousarray(d["L0_attn_Wk"], dtype=np.float32),
        "wv": np.ascontiguousarray(d["L0_attn_Wv"], dtype=np.float32),
        "wo": np.ascontiguousarray(d["L0_attn_Wo"], dtype=np.float32),
        "wgT": np.ascontiguousarray(np.asarray(d["L0_ffn_w_gate"]).T),
        "wuT": np.ascontiguousarray(np.asarray(d["L0_ffn_w_up"]).T),
        "wdT": np.ascontiguousarray(np.asarray(d["L0_ffn_w_down"]).T),
        "sgT": np.ascontiguousarray(np.asarray(d["L0_ffn_s_gate"], dtype=np.float32).reshape(IC, 128).T),
        "suT": np.ascontiguousarray(np.asarray(d["L0_ffn_s_up"], dtype=np.float32).reshape(IC, 128).T),
        "sdT": np.ascontiguousarray(np.asarray(d["L0_ffn_s_down"], dtype=np.float32).reshape(DC, 128).T),
        "iden": np.eye(128).astype(bf),
        "ones2": np.ones((128, 128)).astype(bf),
    }

from concourse.bass_utils import run_bass_kernel_spmd

_NC_CACHE = {}


def kernel(**inputs):
    shared = make_shared_inputs(inputs)
    if "nc" not in _NC_CACHE:
        _NC_CACHE["nc"] = build_nc(debug=False)
    nc = _NC_CACHE["nc"]
    in_maps = []
    metas = []
    for c in range(8):
        per, own_rows, bi = make_core_inputs(inputs, c)
        in_maps.append({**shared, **per})
        metas.append((bi, own_rows))
    res = run_bass_kernel_spmd(nc, in_maps, core_ids=list(range(8)))
    out = np.zeros((4, 2048, 4096), np.float32)
    for c, r in enumerate(res.results):
        bi, own_rows = metas[c]
        out[bi][own_rows] = np.asarray(r["out_T"]).T
    return out


# version 08
# speedup vs baseline: 1.2401x; 1.0157x over previous
import numpy as np
import ml_dtypes
import contextlib
import concourse.bass as bass
import concourse.mybir as mybir

F32 = mybir.dt.float32
BF16 = mybir.dt.bfloat16
I32 = mybir.dt.int32
AF = mybir.ActivationFunctionType
ALU = mybir.AluOpType
AX = mybir.AxisListType.X

D = 4096
T = 2048
NOWN = 1024
DC = 32
NH = 32
KVH = 8
HD = 128
NI = 11008
IC = 86
QSZ = [22, 22, 21, 21]
QBASE = [0, 22, 44, 65]
SCALE = float(1.0 / np.sqrt(HD))
EPS = 1e-5
NPAIRS = [4, 8]


class Sem:
    def __init__(self, h):
        self.h = h
        self.v = 0


def build_nc(debug=False):
    nc = bass.Bass("TRN2", target_bir_lowering=False)

    x_seq = nc.dram_tensor("x_seq", [T, D], F32, kind="ExternalInput").ap()
    x_ownT = nc.dram_tensor("x_ownT", [D, NOWN], F32, kind="ExternalInput").ap()
    wq = nc.dram_tensor("wq", [D, D], F32, kind="ExternalInput").ap()
    wk = nc.dram_tensor("wk", [D, KVH * HD], F32, kind="ExternalInput").ap()
    wv = nc.dram_tensor("wv", [D, KVH * HD], F32, kind="ExternalInput").ap()
    wo = nc.dram_tensor("wo", [D, D], F32, kind="ExternalInput").ap()
    wgT = nc.dram_tensor("wgT", [D, NI], I32, kind="ExternalInput").ap()
    wuT = nc.dram_tensor("wuT", [D, NI], I32, kind="ExternalInput").ap()
    wdT = nc.dram_tensor("wdT", [NI, D], I32, kind="ExternalInput").ap()
    sgT = nc.dram_tensor("sgT", [128, IC], F32, kind="ExternalInput").ap()
    suT = nc.dram_tensor("suT", [128, IC], F32, kind="ExternalInput").ap()
    sdT = nc.dram_tensor("sdT", [128, DC], F32, kind="ExternalInput").ap()
    cosq = nc.dram_tensor("cosq", [64, NOWN], BF16, kind="ExternalInput").ap()
    sinq = nc.dram_tensor("sinq", [64, NOWN], BF16, kind="ExternalInput").ap()
    cosk = nc.dram_tensor("cosk", [64, T], BF16, kind="ExternalInput").ap()
    sink = nc.dram_tensor("sink", [64, T], BF16, kind="ExternalInput").ap()
    masks = nc.dram_tensor("masks", [128, 256], BF16, kind="ExternalInput").ap()
    iden_in = nc.dram_tensor("iden", [128, 128], BF16, kind="ExternalInput").ap()
    ones_in = nc.dram_tensor("ones2", [128, 128], BF16, kind="ExternalInput").ap()

    out_T = nc.dram_tensor("out_T", [D, NOWN], F32, kind="ExternalOutput").ap()

    dbg = "ExternalOutput" if debug else "Internal"
    kT_hbm = nc.dram_tensor("kT_hbm", [KVH, 128, T], BF16, kind=dbg).ap()
    v_hbm = nc.dram_tensor("v_hbm", [KVH, 16, 128, 128], BF16, kind=dbg).ap()
    sT_hbm = nc.dram_tensor("sT_hbm", [D, NOWN], BF16, kind=dbg).ap()
    sp_hbm = nc.dram_tensor("sp_hbm", [2, D, NOWN], BF16, kind=dbg).ap()
    if debug:
        xn_dbg = nc.dram_tensor("xn_dbg", [128, DC * NOWN], BF16, kind="ExternalOutput").ap()
        ctx_dbg = nc.dram_tensor("ctx_dbg", [128, NH * NOWN], BF16, kind="ExternalOutput").ap()
        sffn_dbg = nc.dram_tensor("sffn_dbg", [128, DC * NOWN], BF16, kind="ExternalOutput").ap()

    with contextlib.ExitStack() as _stack:
        ec = _stack.enter_context
        R1 = ec(nc.sbuf_tensor("R1", [128, 32 * 1024], BF16))
        R2 = ec(nc.sbuf_tensor("R2", [128, 32 * 1024], BF16))
        R3 = ec(nc.sbuf_tensor("R3", [128, 22 * 1024], BF16))
        WB = ec(nc.sbuf_tensor("WB", [128, 16 * 1024], BF16))
        mask_t = ec(nc.sbuf_tensor("mask_t", [128, 256], BF16))
        iden_t = ec(nc.sbuf_tensor("iden_t", [128, 128], BF16))
        ones_t = ec(nc.sbuf_tensor("ones_t", [128, 128], BF16))
        sgT_t = ec(nc.sbuf_tensor("sgT_t", [128, IC], F32))
        suT_t = ec(nc.sbuf_tensor("suT_t", [128, IC], F32))
        sdT_t = ec(nc.sbuf_tensor("sdT_t", [128, DC], F32))
        stats = ec(nc.sbuf_tensor("stats", [128, 16], F32))
        PS = ec(nc.psum_tensor("PS", [128, 4096], F32))
        sem_names = ["s_pe", "s_ac", "s_dv", "s_cst", "s_x0", "s_x1",
                     "s_kv0", "s_kv1", "s_kst", "s_vst", "s_wq0", "s_wq1",
                     "s_kk0", "s_kk1", "s_kvv", "s_wd0", "s_wd1", "s_spv",
                     "s_sp0", "s_sp1", "s_ss0", "s_ss1", "s_f0", "s_f1",
                     "s_f2", "s_f3", "s_out", "s_ou1", "s_ou2", "s_ou3",
                     "s_wd2", "s_sp2", "s_sp3", "s_ck", "s_dbg"]
        SM = {n: Sem(ec(nc.semaphore(n))) for n in sem_names}
        block = ec(nc.Block())

        s_pe, s_ac, s_dv = SM["s_pe"], SM["s_ac"], SM["s_dv"]
        s_cst = SM["s_cst"]
        s_x = [SM["s_x0"], SM["s_x1"]]
        s_kv = [SM["s_kv0"], SM["s_kv1"]]
        s_kst, s_vst = SM["s_kst"], SM["s_vst"]
        s_wq = [SM["s_wq0"], SM["s_wq1"]]
        s_kk = [SM["s_kk0"], SM["s_kk1"]]
        s_kvv = SM["s_kvv"]
        s_wd = [SM["s_wd0"], SM["s_wd1"], SM["s_wd2"]]
        s_spv = SM["s_spv"]
        s_sp = [SM["s_sp0"], SM["s_sp1"], SM["s_sp2"], SM["s_sp3"]]
        s_ss = [SM["s_ss0"], SM["s_ss1"]]
        s_f = [SM["s_f0"], SM["s_f1"], SM["s_f2"], SM["s_f3"]]
        s_o = [SM["s_out"], SM["s_ou1"], SM["s_ou2"], SM["s_ou3"]]
        s_ck, s_dbg = SM["s_ck"], SM["s_dbg"]

        prog = {"sync": [], "gpsimd": [], "tensor": [], "vector": [], "scalar": []}
        qlast = {}

        def emit(eng, fn, tag=("o",)):
            prog[eng].append(fn)
            qlast[eng] = tag

        def dma(eng_name, out, in_, sem):
            if qlast.get(eng_name) != ("d", id(sem)) and sem.v > 0:
                emit(eng_name, lambda eng, val=sem.v: eng.wait_ge(sem.h, val))
            sem.v += 16
            v = sem.v
            emit(eng_name, lambda eng, out=out, in_=in_: eng.dma_start(out, in_).then_inc(sem.h, 16),
                 tag=("d", id(sem)))
            return v

        def wait(eng_name, sem, val):
            if val > 0:
                emit(eng_name, lambda eng, val=val: eng.wait_ge(sem.h, val))

        def op(eng_name, fn, sem=None):
            if sem is None:
                emit(eng_name, lambda eng, fn=fn: fn(eng))
                return None
            sem.v += 1
            v = sem.v
            emit(eng_name, lambda eng, fn=fn: fn(eng).then_inc(sem.h, 1))
            return v

        def chain(eng_name, fns, sem):
            v = None
            for fn in fns:
                wait(eng_name, sem, sem.v)
                v = op(eng_name, fn, sem)
            return v

        def group(eng_name, fns, sem):
            sem.v += 1
            v = sem.v
            for i, fn in enumerate(fns):
                if i == len(fns) - 1:
                    emit(eng_name, lambda eng, fn=fn: fn(eng).then_inc(sem.h, 1))
                else:
                    emit(eng_name, lambda eng, fn=fn: fn(eng))
            return v

        xn_ownT = R1
        s_ffnT = R1
        xn_qT = [R2[:, 0:DC * 512], R2[:, DC * 512:DC * 1024]]
        ctxT = R2
        hT = R2
        R3f = R3.bitcast(F32)
        xhalf = [[R3[:, 0:2048], R3[:, 2048:4096]],
                 [R3[:, 4096:6144], R3[:, 6144:8192]]]
        xntile2 = [R3[:, 8192:12288], R3[:, 15360:19456]]
        sqout = R3[:, 19456:20480]
        evk = R3[:, 12288:12800]
        evv = R3[:, 12800:13312]
        rk1 = R3[0:64, 13312:13824]
        rk2 = R3[0:64, 13824:14336]
        ck_s = R3[0:64, 14336:14848]
        sk_s = R3[0:64, 14848:15360]
        PT = R3[:, 0:8192]
        PTr = PT.rearrange("p (c q) -> p c q", q=512)
        kv_k = [R3[:, 8192:10240], R3[:, 10240:12288]]
        kv_v = R3[:, 12288:14336]
        qbf = R3[:, 14336:16384]
        cq_t = R3[0:64, 16384:17408]
        sq_t = R3[0:64, 17408:18432]
        rq1 = R3[0:64, 18432:19456]
        rq2 = R3[0:64, 19456:20480]
        rbc = R3f[:, 10240:10752]
        stile = R3[:, 19456:21504]
        wdb = [R3[:, 0:5632], R3[:, 5632:11264], R2[:, 22528:28160]]
        spv = R3[:, 11264:15360]
        gact = R3[:, 15360:17408]
        us_b = R3[:, 17408:19456]
        R2f = R2.bitcast(F32)
        fxb2 = [R2f[:, 14080 + i * 1024:14080 + (i + 1) * 1024] for i in range(2)]
        fsb2 = [R3[:, 19456 + i * 1024:19456 + (i + 1) * 1024] for i in range(2)]
        fgb2 = [R3f[:, 7680 + i * 1024:7680 + (i + 1) * 1024] for i in range(2)]

        def bank(i):
            return PS[:, i * 512:(i + 1) * 512]
        PSb = PS.bitcast(BF16)

        wk_r = wk.rearrange("(c p) h -> p c h", p=128)
        wv_r = wv.rearrange("(c p) h -> p c h", p=128)
        wq_r = wq.rearrange("(c p) h -> p c h", p=128)
        wo_r = wo.rearrange("(c p) h -> p c h", p=128)
        wgT_r = wgT.rearrange("(c p) i -> p c i", p=128)
        wuT_r = wuT.rearrange("(c p) i -> p c i", p=128)
        wdT_r = wdT.rearrange("(c p) h -> p c h", p=128)
        x_blocks = x_seq.rearrange("(b p) d -> b p d", p=128)

        wkb = [WB[:, 0:4096], WB[:, 4096:8192]]
        wvb = [WB[:, 8192:12288], WB[:, 12288:16384]]
        wqb = [WB[:, 0:4096], WB[:, 4096:8192]]
        wob = wqb
        wgb = [WB[:, 0:4096], WB[:, 4096:8192]]
        wub = [WB[:, 8192:12288], WB[:, 12288:16384]]

        for dst, src in ((mask_t[:], masks), (iden_t[:], iden_in), (ones_t[:], ones_in),
                         (sgT_t[:], sgT), (suT_t[:], suT), (sdT_t[:], sdT)):
            dma("sync", dst, src, s_cst)
        v_const = s_cst.v
        wait("tensor", s_cst, v_const)
        wait("scalar", s_cst, v_const)
        wait("vector", s_cst, v_const)

        a_norm_by = {}
        a_sq_by = {}
        v_tr_by = {}
        v_kmm_by = {}
        v_vmm_by = {}
        v_krope_by = {}
        a_vraw_by = {}
        evac_v_bank = {}
        A_state = {"kv_iter": 0, "v_kdma": None, "v_vt": None, "v_vev": None,
                   "v_vdma": None, "v_cksk": 0, "evac_v": None, "xn_ready": {}}

        vh1_by = {}

        def emit_tile_loads(gb):
            xb = xhalf[gb % 2]
            sx = s_x[gb % 2]
            if gb >= 2:
                wait("gpsimd", s_dv, a_norm_by[gb - 2])
            dma("gpsimd", xb[0], x_blocks[gb, :, 0:2048], sx)
            vh1_by[gb] = dma("gpsimd", xb[1], x_blocks[gb, :, 2048:4096], sx)

        def emit_tile(gb):
            qt = gb // 4
            tb = gb % 4
            xb = xhalf[gb % 2]
            xnt = xntile2[gb % 2]
            sx = s_x[gb % 2]
            sb8 = (gb % 2) * 8
            st1 = stats[:, sb8 + 0:sb8 + 1]
            st2 = stats[:, sb8 + 1:sb8 + 2]
            sq4 = [stats[:, sb8 + 2 + k:sb8 + 3 + k] for k in range(4)]
            mean_t = stats[:, sb8 + 6:sb8 + 7]
            rstd_t = stats[:, sb8 + 7:sb8 + 8]
            vh1 = vh1_by[gb]

            wait("vector", sx, vh1)
            chain("vector", [
                lambda eng, xb=xb, st1=st1: eng.reduce_sum(st1, xb[0], axis=AX),
                lambda eng, xb=xb, st2=st2: eng.reduce_sum(st2, xb[1], axis=AX)], s_dv)
            wait("scalar", sx, vh1)
            if gb >= 1:
                wait("scalar", s_ac, a_sq_by[gb - 1])
            a_sq = chain("scalar", [
                (lambda eng, xb=xb, k=k, s4=sq4[k]: eng.activation(
                    sqout[:], xb[k // 2][:, (k % 2) * 1024:(k % 2 + 1) * 1024],
                    AF.Square, accum_out=s4)) for k in range(4)], s_ac)
            a_sq_by[gb] = a_sq
            wait("vector", s_ac, a_sq)
            v_var = chain("vector", [
                lambda eng, st1=st1, st2=st2: eng.tensor_add(st1, st1, st2),
                lambda eng, st1=st1, mean_t=mean_t: eng.tensor_scalar(mean_t, st1, 1.0 / D, None, ALU.mult),
                lambda eng, s4=sq4: eng.tensor_add(s4[0], s4[0], s4[1]),
                lambda eng, s4=sq4: eng.tensor_add(s4[2], s4[2], s4[3]),
                lambda eng, s4=sq4: eng.tensor_add(s4[0], s4[0], s4[2]),
                lambda eng, s4=sq4: eng.tensor_scalar(s4[0], s4[0], 1.0 / D, None, ALU.mult),
                lambda eng, st2=st2, mean_t=mean_t: eng.tensor_mul(st2, mean_t, mean_t),
                lambda eng, s4=sq4, st2=st2: eng.tensor_scalar(s4[0], s4[0], st2, float(EPS), ALU.subtract, ALU.add)], s_dv)
            wait("scalar", s_dv, v_var)
            a_rstd = chain("scalar", [
                lambda eng, s4=sq4: eng.activation(s4[1], s4[0], AF.Ln),
                lambda eng, s4=sq4, rstd_t=rstd_t: eng.activation(rstd_t, s4[1], AF.Exp, scale=-0.5)], s_ac)
            wait("vector", s_ac, a_rstd)
            if gb >= 2:
                wait("vector", s_pe, v_tr_by[gb - 2])
            a_norm = chain("vector", [
                lambda eng, xb=xb, xnt=xnt, mean_t=mean_t, rstd_t=rstd_t: eng.tensor_scalar(xnt[:, 0:2048], xb[0], mean_t, rstd_t, ALU.subtract, ALU.mult),
                lambda eng, xb=xb, xnt=xnt, mean_t=mean_t, rstd_t=rstd_t: eng.tensor_scalar(xnt[:, 2048:4096], xb[1], mean_t, rstd_t, ALU.subtract, ALU.mult)], s_dv)
            a_norm_by[gb] = a_norm

            wait("tensor", s_dv, a_norm)
            v_tr = None
            for cg in range(4):
                bk = cg % 2
                if bk in evac_v_bank:
                    wait("tensor", s_dv, evac_v_bank[bk])
                fns = []
                for cc in range(8):
                    c = cg * 8 + cc
                    fns.append(lambda eng, bk=bk, cc=cc, c=c, xnt=xnt: eng.transpose(
                        PSb[:, bk * 1024 + cc * 128: bk * 1024 + (cc + 1) * 128],
                        xnt[:, c * 128:(c + 1) * 128], iden_t[:]))
                v_tr = group("tensor", fns, s_pe)
                wait("vector", s_pe, v_tr)
                dst = xn_qT[qt % 2].rearrange("p (c t) -> p c t", t=512)[:, cg * 8:cg * 8 + 8, tb * 128:(tb + 1) * 128]
                src = PSb[:, bk * 1024: bk * 1024 + 1024].rearrange("p (c t) -> p c t", t=128)
                if gb % 2 == 1:
                    op("vector", lambda eng, dst=dst, src=src: eng.tensor_copy(dst, src))
                    dsto = xn_ownT.rearrange("p (c t) -> p c t", t=1024)[:, cg * 8:cg * 8 + 8, (gb // 2) * 128:(gb // 2) * 128 + 128]
                    evac_v = op("vector", lambda eng, dsto=dsto, src=src: eng.tensor_copy(dsto, src), s_dv)
                else:
                    evac_v = op("vector", lambda eng, dst=dst, src=src: eng.tensor_copy(dst, src), s_dv)
                evac_v_bank[bk] = evac_v
            v_tr_by[gb] = v_tr
            if tb == 3:
                A_state["xn_ready"][qt] = evac_v

        def emit_kv(qt, h):
            kv_iter = A_state["kv_iter"]
            bsel = kv_iter % 2
            skv = s_kv[bsel]
            if kv_iter >= 2:
                wait("gpsimd", s_pe, v_vmm_by[kv_iter - 2])
            dma("gpsimd", wkb[bsel].rearrange("p (c j) -> p c j", j=128),
                wk_r[:, :, h * 128:(h + 1) * 128], skv)
            vv_ = dma("gpsimd", wvb[bsel].rearrange("p (c j) -> p c j", j=128),
                      wv_r[:, :, h * 128:(h + 1) * 128], skv)
            kb = 2 + kv_iter % 2
            vb = 4 + kv_iter % 2
            wait("tensor", skv, vv_)
            wait("tensor", s_dv, A_state["xn_ready"][qt])
            if kv_iter >= 2:
                wait("tensor", s_dv, v_krope_by[kv_iter - 2])
            fns = [lambda eng, d=d, kb=kb, bsel=bsel, qt=qt: eng.matmul(
                bank(kb), wkb[bsel][:, d * 128:(d + 1) * 128],
                xn_qT[qt % 2][:, d * 512:(d + 1) * 512],
                start=(d == 0), stop=(d == DC - 1)) for d in range(DC)]
            v_kmm = group("tensor", fns, s_pe)
            v_kmm_by[kv_iter] = v_kmm
            wait("vector", s_pe, v_kmm)
            wait("vector", s_ck, A_state["v_cksk"])
            if A_state["v_kdma"] is not None:
                wait("vector", s_kst, A_state["v_kdma"])
            kq = bank(kb)
            v_krope = chain("vector", [
                lambda eng, kq=kq: eng.tensor_mul(rk1, kq[0:64, :], ck_s),
                lambda eng, kq=kq: eng.tensor_mul(rk2, kq[64:128, :], sk_s),
                lambda eng: eng.tensor_sub(evk[0:64, :], rk1, rk2),
                lambda eng, kq=kq: eng.tensor_mul(rk1, kq[0:64, :], sk_s),
                lambda eng, kq=kq: eng.tensor_mul(rk2, kq[64:128, :], ck_s),
                lambda eng: eng.tensor_add(evk[64:128, :], rk1, rk2)], s_dv)
            v_krope_by[kv_iter] = v_krope
            wait("sync", s_dv, v_krope)
            A_state["v_kdma"] = dma("sync", kT_hbm[h, :, qt * 512:(qt + 1) * 512], evk[:], s_kst)

            if kv_iter >= 2:
                wait("tensor", s_ac, a_vraw_by[kv_iter - 2])
            fns = [lambda eng, d=d, vb=vb, bsel=bsel, qt=qt: eng.matmul(
                bank(vb), wvb[bsel][:, d * 128:(d + 1) * 128],
                xn_qT[qt % 2][:, d * 512:(d + 1) * 512],
                start=(d == 0), stop=(d == DC - 1)) for d in range(DC)]
            v_vmm = group("tensor", fns, s_pe)
            v_vmm_by[kv_iter] = v_vmm
            wait("scalar", s_pe, v_vmm)
            if A_state["v_vt"] is not None:
                wait("scalar", s_pe, A_state["v_vt"])
            if A_state["v_vdma"] is not None:
                wait("scalar", s_vst, A_state["v_vdma"])
            a_vraw = op("scalar", lambda eng, vb=vb: eng.activation(evv[:], bank(vb), AF.Copy), s_ac)
            a_vraw_by[kv_iter] = a_vraw
            wait("tensor", s_ac, a_vraw)
            if A_state["v_vev"] is not None:
                wait("tensor", s_dv, A_state["v_vev"])
            fns = [lambda eng, j=j: eng.transpose(
                PSb[:, 6 * 1024 + j * 128:6 * 1024 + (j + 1) * 128],
                evv[:, j * 128:(j + 1) * 128], iden_t[:]) for j in range(4)]
            v_vt = group("tensor", fns, s_pe)
            A_state["v_vt"] = v_vt
            wait("vector", s_pe, v_vt)
            if A_state["v_vdma"] is not None:
                wait("vector", s_vst, A_state["v_vdma"])
            v_vev = op("vector", lambda eng: eng.tensor_copy(evv[:], PSb[:, 6 * 1024:6 * 1024 + 512]), s_dv)
            A_state["v_vev"] = v_vev
            wait("sync", s_dv, v_vev)
            A_state["v_vdma"] = dma("sync", v_hbm[h, qt * 4:(qt + 1) * 4, :, :].rearrange("j p t -> p j t"),
                                    evv[:].rearrange("p (j t) -> p j t", t=128), s_vst)
            A_state["kv_iter"] += 1

        dma("sync", ck_s, cosk[:, 0:512], s_ck)
        A_state["v_cksk"] = dma("sync", sk_s, sink[:, 0:512], s_ck)
        emit_tile_loads(0)
        emit_tile_loads(1)
        for tb in range(4):
            emit_tile(tb)
            if tb + 2 < 4:
                emit_tile_loads(tb + 2)
        for qt in range(4):
            if qt >= 1:
                wait("sync", s_dv, v_krope_by[A_state["kv_iter"] - 1])
                dma("sync", ck_s, cosk[:, qt * 512:(qt + 1) * 512], s_ck)
                A_state["v_cksk"] = dma("sync", sk_s, sink[:, qt * 512:(qt + 1) * 512], s_ck)
            for h in range(KVH):
                if qt < 3 and h % 2 == 0:
                    emit_tile_loads(4 * (qt + 1) + h // 2)
                emit_kv(qt, h)
                if qt < 3 and h % 2 == 1:
                    emit_tile(4 * (qt + 1) + (h - 1) // 2)
        kv_iter = A_state["kv_iter"]
        wdb0_q = R3[:, 0:4096]
        wait("gpsimd", s_dv, a_norm_by[14])
        v_wq0pre = dma("gpsimd", wdb0_q.rearrange("p (c j) -> p c j", j=128),
                       wq_r[:, :, 0:128], s_wd[0])
        v_xndbg = 0
        if debug:
            wait("sync", s_dv, s_dv.v)
            v_xndbg = dma("sync", xn_dbg, xn_ownT[:], s_dbg)

        wait("tensor", s_dv, s_dv.v)
        wait("tensor", s_ac, s_ac.v)
        v_wq = [0, 0]
        v_wo = [0, 0]
        v_wg = {}
        v_wu = {}
        wait("gpsimd", s_pe, v_vmm_by[kv_iter - 1])
        v_wq[1] = dma("gpsimd", wqb[1].rearrange("p (c j) -> p c j", j=128),
                      wq_r[:, :, 128:256], s_wq[1])
        v_wq[0] = dma("gpsimd", wqb[0].rearrange("p (c j) -> p c j", j=128),
                      wq_r[:, :, 256:384], s_wq[0])
        v_wu[0] = dma("gpsimd", wub[0].rearrange("p (c j) -> p c j", j=128),
                      wuT_r[:, :, 0:128], s_kv[0])
        v_wu[1] = dma("gpsimd", wub[1].rearrange("p (c j) -> p c j", j=128),
                      wuT_r[:, :, 128:256], s_kv[1])
        wait("sync", s_kst, s_kst.v)
        wait("sync", s_vst, s_vst.v)
        wait("sync", s_dv, v_krope_by[kv_iter - 1])
        dma("sync", cq_t, cosq, s_ck)
        v_cq = dma("sync", sq_t, sinq, s_ck)
        v_kv = [0, 0]
        v_kv[0] = dma("sync", kv_k[0][:], kT_hbm[0, :, :], s_kk[0])
        v_kvv = dma("sync", kv_v[:].rearrange("p (j t) -> p j t", t=128),
                    v_hbm[0, :, :, :].rearrange("j p t -> p j t"), s_kvv)

        def ql_of(g, jp):
            return max(0, jp - 4 * g) * 128

        v_qproj_by = {}
        v_qrope_by = {}
        v_attn_done_by = {}
        a_exp_by = {}
        piter_box = [0]

        def q_proj(h):
            hb = h % 2
            if h == 0:
                wait("tensor", s_wd[0], v_wq0pre)
                lhs = wdb0_q
            else:
                wait("tensor", s_wq[hb], v_wq[hb])
                lhs = wqb[hb]
            if h >= 1:
                wait("tensor", s_dv, v_qrope_by[h - 1])
            fns = []
            for d in range(DC):
                for half in range(2):
                    fns.append(lambda eng, d=d, half=half, lhs=lhs: eng.matmul(
                        PS[:, half * 512:(half + 1) * 512],
                        lhs[:, d * 128:(d + 1) * 128],
                        xn_ownT[:, d * 1024 + half * 512: d * 1024 + (half + 1) * 512],
                        start=(d == 0), stop=(d == DC - 1)))
            v = group("tensor", fns, s_pe)
            v_qproj_by[h] = v
            return v

        def q_rope_start(h, v_qmm):
            hb = h % 2
            q1 = PS[0:64, 0:1024]
            q2 = PS[64:128, 0:1024]
            ops = [
                lambda eng: eng.tensor_mul(rq1, q1, cq_t),
                lambda eng: eng.tensor_mul(rq2, q2, sq_t),
                lambda eng, hb=hb: eng.tensor_sub(qbf[0:64, hb * 1024:(hb + 1) * 1024], rq1, rq2),
                lambda eng: eng.tensor_mul(rq1, q1, sq_t),
                lambda eng: eng.tensor_mul(rq2, q2, cq_t),
                lambda eng, hb=hb: eng.tensor_add(qbf[64:128, hb * 1024:(hb + 1) * 1024], rq1, rq2)]
            return {"i": 0, "ops": ops, "v": None, "vq": v_qmm, "h": h}

        def rope_step(st, n=1):
            if st is None:
                return
            for _ in range(n):
                i = st["i"]
                if i >= 6:
                    return
                if i == 0:
                    wait("vector", s_ck, v_cq)
                    wait("vector", s_pe, st["vq"])
                    if st["h"] >= 2 and (st["h"] - 2) in v_attn_done_by:
                        wait("vector", s_pe, v_attn_done_by[st["h"] - 2])
                else:
                    wait("vector", s_dv, st["v"])
                st["v"] = op("vector", st["ops"][i], s_dv)
                st["i"] += 1
                if st["i"] == 6:
                    v_qrope_by[st["h"]] = st["v"]

        v0 = q_proj(0)
        rope_step(q_rope_start(0, v0), 6)
        v_ctx_prev_head = None

        for h in range(NH):
            hb = h % 2
            kvh = h // 4
            kvb = kvh % 2
            if h % 4 == 0 and kvh + 1 < KVH:
                if kvh >= 1:
                    wait("sync", s_pe, v_attn_done_by[4 * kvh - 1])
                v_kv[(kvb + 1) % 2] = dma("sync", kv_k[(kvb + 1) % 2][:], kT_hbm[kvh + 1, :, :], s_kk[(kvb + 1) % 2])
            rs_next = None
            if h + 1 < NH:
                v_next = q_proj(h + 1)
                rs_next = q_rope_start(h + 1, v_next)
            if h + 3 < NH:
                wait("gpsimd", s_pe, v_qproj_by[h + 1])
                v_wq[(h + 3) % 2] = dma("gpsimd", wqb[(h + 3) % 2].rearrange("p (c j) -> p c j", j=128),
                                        wq_r[:, :, (h + 3) * 128:(h + 4) * 128], s_wq[(h + 3) % 2])
            elif h + 3 == NH:
                wait("gpsimd", s_pe, v_qproj_by[h + 1])
                v_wo[0] = dma("gpsimd", wob[0].rearrange("p (c j) -> p c j", j=128),
                              wo_r[:, :, 0:128], s_wq[0])

            wait("tensor", s_kk[kvb], v_kv[kvb])
            wait("tensor", s_kvv, v_kvv)
            wait("tensor", s_dv, v_qrope_by[h])
            v_ctx = v_ctx_prev_head
            v_grp = None
            for g in range(2):
                np_ = NPAIRS[g]
                pend_q = []

                def flush_pend(p):
                    if p is None:
                        return None
                    for wfn in p[0]:
                        wfn()
                    return group("tensor", p[1], s_pe)

                for jp in range(np_):
                    piter = piter_box[0]
                    ql = ql_of(g, jp)
                    stb = 2 + (piter % 2) * 2
                    if piter >= 2:
                        wait("tensor", s_ac, a_exp_by[piter - 2])
                    fns = []
                    for m in range(2):
                        j = 2 * jp + m
                        fns.append(lambda eng, j=j, m=m, ql=ql, g=g, hb=hb, kvb=kvb, stb=stb: eng.matmul(
                            PS[:, (stb + m) * 512 + ql: (stb + m + 1) * 512],
                            kv_k[kvb][:, j * 128:(j + 1) * 128],
                            qbf[:, hb * 1024 + g * 512 + ql: hb * 1024 + (g + 1) * 512],
                            start=True, stop=True))
                    v_st = group("tensor", fns, s_pe)
                    if len(pend_q) == 2:
                        v_grp = flush_pend(pend_q.pop(0))
                    wait("scalar", s_pe, v_st)

                    def fexp(eng, jp=jp, ql=ql, stb=stb):
                        return eng.activation(
                            PTr[:, 2 * jp:2 * jp + 2, ql:512],
                            PS[:, stb * 512:(stb + 2) * 512].rearrange("p (c q) -> p c q", q=512)[:, :, ql:512],
                            AF.Exp, scale=SCALE)
                    a_exp = op("scalar", fexp, s_ac)
                    a_exp_by[piter] = a_exp
                    pp = jp - 4 * g
                    is_masked = (g == 0) or (jp >= 4)
                    if is_masked:
                        wait("vector", s_ac, a_exp)

                        def fmask(eng, jp=jp, pp=pp):
                            return eng.tensor_mul(
                                PTr[:, 2 * jp:2 * jp + 2, pp * 128:(pp + 1) * 128],
                                PTr[:, 2 * jp:2 * jp + 2, pp * 128:(pp + 1) * 128],
                                mask_t[:].rearrange("p (c q) -> p c q", q=128))
                        v_mask = op("vector", fmask, s_dv)
                        if (g == 0 and jp in (1, 3)) or (g == 1 and jp in (4, 5)):
                            rope_step(rs_next, 1)
                        deps = [lambda vm=v_mask: wait("tensor", s_dv, vm)]
                    else:
                        deps = [lambda ae=a_exp: wait("tensor", s_ac, ae)]
                    if jp == 0 and v_ctx is not None:
                        deps.append(lambda vc=v_ctx: wait("tensor", s_dv, vc))
                    fns = []
                    for m in range(2):
                        j = 2 * jp + m
                        fns.append(lambda eng, j=j, ql=ql, g=g: eng.matmul(
                            PS[:, 6 * 512 + ql:7 * 512],
                            ones_t[:],
                            PT[:, j * 512 + ql:(j + 1) * 512],
                            start=(j == 0), stop=(j == 2 * NPAIRS[g] - 1)))
                        fns.append(lambda eng, j=j, ql=ql, g=g: eng.matmul(
                            PS[:, 7 * 512 + ql:8 * 512],
                            kv_v[:, j * 128:(j + 1) * 128],
                            PT[:, j * 512 + ql:(j + 1) * 512],
                            start=(j == 0), stop=(j == 2 * NPAIRS[g] - 1)))
                    pend_q.append((deps, fns))
                    piter_box[0] += 1
                for _p in pend_q:
                    v_grp = flush_pend(_p)
                pend_q = []
                wait("vector", s_pe, v_grp)

                def fev(eng, h=h, g=g):
                    return eng.tensor_mul(
                        ctxT[:, h * 1024 + g * 512: h * 1024 + (g + 1) * 512],
                        bank(7), rbc)
                v_ctx = chain("vector", [
                    lambda eng: eng.reciprocal(rbc, bank(6)), fev], s_dv)
                if g == 0:
                    rope_step(rs_next, 2)
            v_attn_done_by[h] = v_grp
            rope_step(rs_next, 6)
            v_ctx_prev_head = v_ctx
            if h % 4 == 3 and kvh + 1 < KVH:
                wait("sync", s_pe, v_grp)
                v_kvv = dma("sync", kv_v[:].rearrange("p (j t) -> p j t", t=128),
                            v_hbm[kvh + 1, :, :, :].rearrange("j p t -> p j t"), s_kvv)

        v_ctxdbg = 0
        if debug:
            wait("sync", s_dv, s_dv.v)
            v_ctxdbg = dma("sync", ctx_dbg, ctxT[:], s_dbg)

        wait("tensor", s_dv, s_dv.v)
        wait("tensor", s_ac, s_ac.v)
        v_sdma_by = {}
        a_sev_by = {}
        v_smm_by = {}
        for hc in range(DC):
            hb = hc % 2
            if hc + 1 < DC:
                wait("gpsimd", s_pe, v_smm_by[hc - 1] if hc >= 1 else v_qproj_by[NH - 1])
                v_wo[(hc + 1) % 2] = dma("gpsimd", wob[(hc + 1) % 2].rearrange("p (c j) -> p c j", j=128),
                                         wo_r[:, :, (hc + 1) * 128:(hc + 2) * 128], s_wq[(hc + 1) % 2])
            bk0 = (hc % 2) * 2
            wait("tensor", s_wq[hb], v_wo[hb])
            if hc >= 2:
                wait("tensor", s_ac, a_sev_by[hc - 2])
            fns = []
            for d in range(DC):
                for half in range(2):
                    fns.append(lambda eng, d=d, half=half, hb=hb, bk0=bk0: eng.matmul(
                        PS[:, (bk0 + half) * 512: (bk0 + half + 1) * 512],
                        wob[hb][:, d * 128:(d + 1) * 128],
                        ctxT[:, d * 1024 + half * 512: d * 1024 + (half + 1) * 512],
                        start=(d == 0), stop=(d == DC - 1)))
            v_smm = group("tensor", fns, s_pe)
            v_smm_by[hc] = v_smm
            if hc == DC - 2:
                wait("gpsimd", s_pe, v_smm)
                v_wg[0] = dma("gpsimd", wgb[0].rearrange("p (c j) -> p c j", j=128),
                              wgT_r[:, :, 0:128], s_wq[0])
            wait("scalar", s_pe, v_smm)
            if hc >= 2:
                wait("scalar", s_ss[hb], v_sdma_by[hc - 2])
            a_sev = op("scalar", lambda eng, hb=hb, bk0=bk0: eng.activation(
                stile[:, hb * 1024:(hb + 1) * 1024], PS[:, bk0 * 512:(bk0 + 2) * 512], AF.Copy), s_ac)
            a_sev_by[hc] = a_sev
            wait("sync", s_ac, a_sev)
            v_sdma_by[hc] = dma("sync", sT_hbm[hc * 128:(hc + 1) * 128, :],
                                stile[:, hb * 1024:(hb + 1) * 1024], s_ss[hb])

        def prefetch_gu(i, bsel):
            v_wg[i] = dma("gpsimd", wgb[bsel].rearrange("p (c j) -> p c j", j=128),
                          wgT_r[:, :, i * 128:(i + 1) * 128], s_wq[bsel])
            v_wu[i] = dma("gpsimd", wub[bsel].rearrange("p (c j) -> p c j", j=128),
                          wuT_r[:, :, i * 128:(i + 1) * 128], s_kv[bsel])

        wait("gpsimd", s_pe, v_smm_by[DC - 1])
        wait("gpsimd", s_dv, s_dv.v)
        wait("sync", s_dv, s_dv.v)
        v_wg[1] = dma("gpsimd", wgb[1].rearrange("p (c j) -> p c j", j=128),
                      wgT_r[:, :, 128:256], s_wq[1])
        v_hT = {}
        a_g_by = {}
        v_u_by = {}
        wglob_box = [0]
        vwd_by = {}
        v_half_mms = {}
        eslot = [stile[:, 0:1024], stile[:, 1024:2048], gact[:, 0:1024], gact[:, 1024:2048]]
        pend_stores = []
        giter = 0
        v_dn_prev = None
        v_inj_prev = None
        eidx = 0
        v_spdma_by = {}

        v_fo_by = {}
        v_out_by = {}
        v_fs_by = {}
        ev_ac_box = [0]
        ev_dv_box = [0]
        ev_gate = {}
        deferred_pf = []

        def f_loads(hc):
            b = hc % 2
            if hc == 0:
                wait("gpsimd", s_ss[0], s_ss[0].v)
                wait("gpsimd", s_ss[1], s_ss[1].v)
                for _q in range(4):
                    wait("gpsimd", s_sp[_q], s_sp[_q].v)
            elif hc >= 2:
                wait("gpsimd", s_dv, v_fo_by[hc - 2])
                wait("gpsimd", s_o[b], v_out_by[hc - 2])
            dma("gpsimd", fxb2[b], x_ownT[hc * 128:(hc + 1) * 128, :], s_f[b])
            v_fs_by[hc] = dma("gpsimd", fsb2[b], sT_hbm[hc * 128:(hc + 1) * 128, :], s_f[b])

        def f_chunk(hc):
            b = hc % 2
            if hc == 0:
                wait("scalar", s_dv, v_hT[IC - 1])
                for _q in range(4):
                    wait("scalar", s_sp[_q], s_sp[_q].v)
            elif hc >= 2:
                wait("scalar", s_dv, v_fo_by[hc - 2])
            esem, ev = ev_gate[hc]
            wait("scalar", esem, ev)
            a_si = op("scalar", lambda eng, hc=hc, b=b: eng.activation(
                fgb2[b], s_ffnT[:, hc * 1024:(hc + 1) * 1024], AF.Silu), s_ac)
            wait("vector", s_ac, a_si)
            wait("vector", s_f[b], v_fs_by[hc])
            v_fo_by[hc] = chain("vector", [
                lambda eng, b=b: eng.tensor_mul(fgb2[b], fsb2[b], fgb2[b]),
                lambda eng, b=b: eng.tensor_add(fxb2[b], fxb2[b], fgb2[b])], s_dv)
            wait("sync", s_dv, v_fo_by[hc])
            v_out_by[hc] = dma("sync", out_T[hc * 128:(hc + 1) * 128, :], fxb2[b], s_o[b])
            if hc + 2 < DC:
                f_loads(hc + 2)
        for qq in range(4):
            base, sz = QBASE[qq], QSZ[qq]
            for ii in range(sz):
                i = base + ii
                bsel = giter % 2
                gbk = (giter % 2) * 2
                ubk = 4 + (giter % 2) * 2
                wait("tensor", s_wq[bsel], v_wg[i])
                wait("tensor", s_kv[bsel], v_wu[i])
                if ii == 0:
                    wait("tensor", s_ac, s_ac.v)
                    wait("tensor", s_dv, s_dv.v)
                if giter >= 2 and ii >= 2:
                    wait("tensor", s_dv, v_hT[i - 2])
                    wait("tensor", s_ac, a_g_by[i - 2])
                elif giter >= 2 and ii == 1:
                    wait("tensor", s_dv, v_hT[i - 2])
                fns = []
                for d in range(DC):
                    for half in range(2):
                        fns.append(lambda eng, d=d, half=half, bsel=bsel, gbk=gbk: eng.matmul(
                            PS[:, (gbk + half) * 512:(gbk + half + 1) * 512],
                            wgb[bsel][:, d * 128:(d + 1) * 128],
                            xn_ownT[:, d * 1024 + half * 512: d * 1024 + (half + 1) * 512],
                            start=(d == 0), stop=(d == DC - 1)))
                v_g = group("tensor", fns, s_pe)
                fns = []
                for d in range(DC):
                    for half in range(2):
                        fns.append(lambda eng, d=d, half=half, bsel=bsel, ubk=ubk: eng.matmul(
                            PS[:, (ubk + half) * 512:(ubk + half + 1) * 512],
                            wub[bsel][:, d * 128:(d + 1) * 128],
                            xn_ownT[:, d * 1024 + half * 512: d * 1024 + (half + 1) * 512],
                            start=(d == 0), stop=(d == DC - 1)))
                v_u = group("tensor", fns, s_pe)
                v_u_by[i] = v_u
                if i + 2 < IC:
                    if ii >= sz - 2:
                        deferred_pf.append((i + 2, i % 2, v_u))
                    else:
                        wait("gpsimd", s_pe, v_u)
                        prefetch_gu(i + 2, i % 2)
                wait("scalar", s_pe, v_g)
                if giter >= 2:
                    wait("scalar", s_dv, v_hT[i - 2])
                if giter == 0:
                    wait("scalar", s_ss[0], v_sdma_by[DC - 2])
                    wait("scalar", s_ss[1], v_sdma_by[DC - 1])
                    wait("scalar", s_dv, s_dv.v)
                if ii == 0 and qq > 0:
                    for _q in range(4):
                        wait("scalar", s_sp[_q], s_sp[_q].v)
                gsl = giter % 2
                a_g = op("scalar", lambda eng, i=i, gbk=gbk, gsl=gsl: eng.activation(
                    gact[:, gsl * 1024:(gsl + 1) * 1024], PS[:, gbk * 512:(gbk + 2) * 512],
                    AF.Silu, scale=sgT_t[:, i:i + 1]), s_ac)
                wait("scalar", s_pe, v_u)
                a_us = op("scalar", lambda eng, i=i, ubk=ubk, gsl=gsl: eng.activation(
                    us_b[:, gsl * 1024:(gsl + 1) * 1024], PS[:, ubk * 512:(ubk + 2) * 512],
                    AF.Copy, scale=suT_t[:, i:i + 1]), s_ac)
                a_g_by[i] = a_us
                wait("vector", s_ac, a_us)
                if debug and giter == 0:
                    wait("vector", s_dbg, v_ctxdbg)

                def fh(eng, ii=ii, gsl=gsl):
                    return eng.tensor_mul(
                        hT[:, ii * 1024:(ii + 1) * 1024],
                        us_b[:, gsl * 1024:(gsl + 1) * 1024],
                        gact[:, gsl * 1024:(gsl + 1) * 1024])
                v_hT[i] = op("vector", fh, s_dv)
                giter += 1

            par = qq % 2
            nh0 = (sz + 1) // 2
            wbase = wglob_box[0]

            def wd_dma(w, base=base, sz=sz, nh0=nh0):
                hqw, hfw = w // 2, w % 2
                i0 = 0 if hfw == 0 else nh0
                i1 = nh0 if hfw == 0 else sz
                wg_ = wglob_box[0]
                slot = wg_ % 3
                if wg_ >= 3:
                    wait("gpsimd", s_pe, v_half_mms[wg_ - 3])
                else:
                    wait("gpsimd", s_pe, v_smm_by[DC - 1])
                vwd_by[wg_] = dma("gpsimd",
                                  wdb[slot][:, 0:(i1 - i0) * 512].rearrange("p (c j) -> p c j", j=512),
                                  wdT_r[:, base + i0:base + i1, hqw * 512:(hqw + 1) * 512], s_wd[slot])
                wglob_box[0] += 1
                return vwd_by[wg_]

            wd_dma(0)
            wd_dma(1)

            def spv_loads(hq):
                return [dma("sync", spv[:, l * 1024:(l + 1) * 1024],
                            sp_hbm[1 - par, (hq * 4 + l) * 128:(hq * 4 + l + 1) * 128, :], s_spv)
                        for l in range(4)]

            vsp_by = {}
            if qq > 0:
                for _q in range(4):
                    wait("sync", s_sp[_q], s_sp[_q].v)
                if v_inj_prev is not None:
                    wait("sync", s_pe, v_inj_prev)
                vsp_by[0] = spv_loads(0)
            if qq == 3:
                f_loads(0)
                f_loads(1)
            for hq in range(8):
                if hq == 0:
                    wait("tensor", s_dv, v_hT[base + sz - 1])
                if qq == 3 and hq >= 1:
                    wait("tensor", s_ac, ev_ac_box[0])
                    wait("tensor", s_dv, ev_dv_box[0])
                else:
                    wait("tensor", s_ac, s_ac.v)
                    wait("tensor", s_dv, s_dv.v)
                if qq > 0:
                    wait("tensor", s_spv, vsp_by[hq][3])
                    fns = []
                    for l in range(4):
                        for half in range(2):
                            fns.append(lambda eng, l=l, half=half: eng.matmul(
                                PS[:, (l * 2 + half) * 512:(l * 2 + half + 1) * 512],
                                iden_t[:],
                                spv[:, l * 1024 + half * 512: l * 1024 + (half + 1) * 512],
                                start=True, stop=False))
                    v_inj_prev = group("tensor", fns, s_pe)
                    if hq + 1 < 8:
                        wait("sync", s_pe, v_inj_prev)
                        vsp_by[hq + 1] = spv_loads(hq + 1)
                for hf in range(2):
                    i0 = 0 if hf == 0 else nh0
                    i1 = nh0 if hf == 0 else sz
                    w = hq * 2 + hf
                    wg_ = wbase + w
                    slot = wg_ % 3
                    wait("tensor", s_wd[slot], vwd_by[wg_])
                    fns = []
                    for ii in range(i0, i1):
                        for l in range(4):
                            for half in range(2):
                                fns.append(lambda eng, ii=ii, iii=(ii - i0), slot=slot, l=l, half=half, qq=qq,
                                           st=(qq == 0 and ii == 0), sp=(hf == 1 and ii == i1 - 1): eng.matmul(
                                    PS[:, (l * 2 + half) * 512:(l * 2 + half + 1) * 512],
                                    wdb[slot][:, iii * 512 + l * 128: iii * 512 + (l + 1) * 128],
                                    hT[:, ii * 1024 + half * 512: ii * 1024 + (half + 1) * 512],
                                    start=st, stop=sp))
                    v_dn_prev = group("tensor", fns, s_pe)
                    v_half_mms[wg_] = v_dn_prev
                    if w + 2 < 16:
                        wd_dma(w + 2)
                if deferred_pf and hq in (1, 2):
                    pi, pb, pvu = deferred_pf.pop(0)
                    wait("gpsimd", s_pe, pvu)
                    prefetch_gu(pi, pb)
                for (psem, pv, pslot, phc, ppar) in pend_stores:
                    wait("sync", psem, pv)
                    v_spdma_by[eidx] = dma(
                        "sync", sp_hbm[ppar, phc * 128:(phc + 1) * 128, :],
                        eslot[pslot], s_sp[eidx % 4])
                    eidx += 1
                pend_stores = []
                wait("scalar", s_pe, v_dn_prev)
                wait("vector", s_pe, v_dn_prev)
                for l in range(4):
                    hc = hq * 4 + l
                    edev = "scalar" if l < 2 else "vector"
                    esem = s_ac if l < 2 else s_dv
                    if qq < 3:
                        if eidx + len(pend_stores) >= 4:
                            k = eidx + len(pend_stores) - 4
                            wait(edev, s_sp[k % 4], v_spdma_by[k])
                        if l < 2:
                            a_e = op("scalar", lambda eng, l=l: eng.activation(
                                eslot[l],
                                PS[:, l * 2 * 512:(l * 2 + 2) * 512], AF.Copy), s_ac)
                        else:
                            a_e = op("vector", lambda eng, l=l: eng.tensor_copy(
                                eslot[l],
                                PS[:, l * 2 * 512:(l * 2 + 2) * 512]), s_dv)
                        pend_stores.append((esem, a_e, l, hc, par))
                    else:
                        if debug and qq == 3 and hq == 0 and l == 0:
                            wait("scalar", s_dbg, s_dbg.v)
                        if l < 2:
                            ev = op("scalar", lambda eng, l=l, hc=hc: eng.activation(
                                s_ffnT[:, hc * 1024:(hc + 1) * 1024],
                                PS[:, l * 2 * 512:(l * 2 + 2) * 512], AF.Copy,
                                scale=sdT_t[:, hc:hc + 1]), s_ac)
                            ev_ac_box[0] = ev
                        else:
                            ev = op("vector", lambda eng, l=l, hc=hc: eng.tensor_scalar(
                                s_ffnT[:, hc * 1024:(hc + 1) * 1024],
                                PS[:, l * 2 * 512:(l * 2 + 2) * 512],
                                sdT_t[:, hc:hc + 1], None, ALU.mult), s_dv)
                            ev_dv_box[0] = ev
                        ev_gate[hc] = (esem, ev)
                if qq == 3:
                    for l in range(4):
                        f_chunk(hq * 4 + l)
            for (psem, pv, pslot, phc, ppar) in pend_stores:
                wait("sync", psem, pv)
                v_spdma_by[eidx] = dma(
                    "sync", sp_hbm[ppar, phc * 128:(phc + 1) * 128, :],
                    eslot[pslot], s_sp[eidx % 4])
                eidx += 1
            pend_stores = []
        a_ffn_done = s_ac.v
        if debug:
            wait("sync", s_ac, a_ffn_done)
            dma("sync", sffn_dbg, s_ffnT[:], s_dbg)

        for _sm in (s_cst, s_ck, s_kst, s_vst, s_kk[0], s_kk[1], s_kvv, s_spv,
                    s_sp[0], s_sp[1], s_sp[2], s_sp[3], s_ss[0], s_ss[1], s_f[0], s_f[1],
                    s_f[2], s_f[3], s_o[0], s_o[1], s_o[2], s_o[3], s_dbg):
            wait("sync", _sm, _sm.v)
        for _sm in (s_x[0], s_x[1], s_kv[0], s_kv[1], s_wq[0], s_wq[1],
                    s_wd[0], s_wd[1], s_wd[2]):
            wait("gpsimd", _sm, _sm.v)

        @block.sync
        def _(eng):
            for f in prog["sync"]:
                f(eng)

        @block.gpsimd
        def _(eng):
            for f in prog["gpsimd"]:
                f(eng)

        @block.tensor
        def _(eng):
            for f in prog["tensor"]:
                f(eng)

        @block.vector
        def _(eng):
            for f in prog["vector"]:
                f(eng)

        @block.scalar
        def _(eng):
            for f in prog["scalar"]:
                f(eng)

    return nc


def make_core_inputs(d, core):
    bi, rho = core // 2, core % 2
    x = np.asarray(d["L0_attn_x"])
    NB = 16
    if rho == 0:
        order = []
        for p in range(NB // 2):
            order += [2 * p + 1, 2 * p]
    else:
        order = list(range(NB))
    xs = x[bi].reshape(NB, 128, D)[order].reshape(T, D)
    slot_pos = np.concatenate([np.arange(o * 128, o * 128 + 128) for o in order])
    own_idx = np.concatenate([np.arange(s * 128, s * 128 + 128) for s in range(1, NB, 2)])
    own_rows = slot_pos[own_idx]

    bf = ml_dtypes.bfloat16
    inv = (1.0 / 500000.0 ** (np.arange(0, HD, 2) / HD)).astype(np.float64)
    ang_k = slot_pos[:, None] * inv[None, :]
    cos_k = np.cos(ang_k).T.astype(bf)
    sin_k = np.sin(ang_k).T.astype(bf)
    ang_q = own_rows[:, None] * inv[None, :]
    cos_q = np.cos(ang_q).T.astype(bf)
    sin_q = np.sin(ang_q).T.astype(bf)

    m_even = (np.ones((128, 128)) if rho == 1 else np.zeros((128, 128))).astype(bf)
    m_diag = np.triu(np.ones((128, 128))).astype(bf)
    masks_np = np.concatenate([m_even, m_diag], axis=1)

    return {
        "x_seq": np.ascontiguousarray(xs, dtype=np.float32),
        "x_ownT": np.ascontiguousarray(x[bi][own_rows].T, dtype=np.float32),
        "cosq": np.ascontiguousarray(cos_q), "sinq": np.ascontiguousarray(sin_q),
        "cosk": np.ascontiguousarray(cos_k), "sink": np.ascontiguousarray(sin_k),
        "masks": np.ascontiguousarray(masks_np),
    }, own_rows, bi


def make_shared_inputs(d):
    bf = ml_dtypes.bfloat16
    return {
        "wq": np.ascontiguousarray(d["L0_attn_Wq"], dtype=np.float32),
        "wk": np.ascontiguousarray(d["L0_attn_Wk"], dtype=np.float32),
        "wv": np.ascontiguousarray(d["L0_attn_Wv"], dtype=np.float32),
        "wo": np.ascontiguousarray(d["L0_attn_Wo"], dtype=np.float32),
        "wgT": np.ascontiguousarray(np.asarray(d["L0_ffn_w_gate"]).T),
        "wuT": np.ascontiguousarray(np.asarray(d["L0_ffn_w_up"]).T),
        "wdT": np.ascontiguousarray(np.asarray(d["L0_ffn_w_down"]).T),
        "sgT": np.ascontiguousarray(np.asarray(d["L0_ffn_s_gate"], dtype=np.float32).reshape(IC, 128).T),
        "suT": np.ascontiguousarray(np.asarray(d["L0_ffn_s_up"], dtype=np.float32).reshape(IC, 128).T),
        "sdT": np.ascontiguousarray(np.asarray(d["L0_ffn_s_down"], dtype=np.float32).reshape(DC, 128).T),
        "iden": np.eye(128).astype(bf),
        "ones2": np.ones((128, 128)).astype(bf),
    }

from concourse.bass_utils import run_bass_kernel_spmd

_NC_CACHE = {}


def kernel(**inputs):
    shared = make_shared_inputs(inputs)
    if "nc" not in _NC_CACHE:
        _NC_CACHE["nc"] = build_nc(debug=False)
    nc = _NC_CACHE["nc"]
    in_maps = []
    metas = []
    for c in range(8):
        per, own_rows, bi = make_core_inputs(inputs, c)
        in_maps.append({**shared, **per})
        metas.append((bi, own_rows))
    res = run_bass_kernel_spmd(nc, in_maps, core_ids=list(range(8)))
    out = np.zeros((4, 2048, 4096), np.float32)
    for c, r in enumerate(res.results):
        bi, own_rows = metas[c]
        out[bi][own_rows] = np.asarray(r["out_T"]).T
    return out
